# version 11
# baseline (speedup 1.0000x reference)
import math
import numpy as np
import ml_dtypes
import concourse.bass as bass
import concourse.mybir as mybir
from concourse.bass_utils import run_bass_kernel_spmd

F32 = mybir.dt.float32
BF16 = mybir.dt.bfloat16
AF = mybir.ActivationFunctionType
ALU = mybir.AluOpType
AX = mybir.AxisListType

D = 1024
DC = D // 128
H = 16
HD = 64
WIN = 7
NE = 8
FH = 4096
DI = 2048
DIC = DI // 128
NST = 16
NCORES = 8

ENG = ["pe", "dve", "act", "pool", "sp"]
DMA_LANES = {"sp": [f"dh{i}" for i in range(4)], "pool": [f"ds{i}" for i in range(6)]}
DMA_SEMS = DMA_LANES["sp"] + DMA_LANES["pool"]


class Res:
    __slots__ = ["name", "writer", "readers"]

    def __init__(self, name):
        self.name = name
        self.writer = None
        self.readers = []


class Bld:
    def __init__(self, nc, sems):
        self.nc = nc
        self.sems = sems
        self.c = {k: 0 for k in sems}
        self.water = {e: {k: 0 for k in sems} for e in ENG}
        self.eng = {"pe": nc.tensor, "dve": nc.vector, "act": nc.scalar,
                    "pool": nc.gpsimd, "sp": nc.sync}
        self.pending = {}
        self.lane_rr = {"sp": 0, "pool": 0}
        self.opn = {e: 0 for e in ENG}
        self.strict = False

    SELF_GAP = 8

    def _wait(self, ename, tok):
        if tok is None:
            return
        sem, cnt = tok[0], tok[1]
        if self.water[ename][sem] >= cnt:
            return
        self.eng[ename].wait_ge(self.sems[sem], cnt)
        self.water[ename][sem] = cnt

    def op(self, ename, fn, reads=(), writes=(), dma=False, inc=True):
        e = self.eng[ename]
        if dma:
            lanes = DMA_LANES[ename]
            my_sem = lanes[self.lane_rr[ename] % len(lanes)]
            self.lane_rr[ename] += 1
            self._wait(ename, (my_sem, self.c[my_sem]))
        else:
            my_sem = ename
        myop = self.opn[ename]
        if self.strict:
            for k, v in self.c.items():
                if v > 0:
                    self._wait(ename, (k, v))

        def need_same_edge(tok):
            if ename == "pe" and tok[0] == "pe":
                return False
            if len(tok) < 3:
                return True
            return myop - tok[2] < self.SELF_GAP

        for r in reads:
            if r.writer is not None and (dma or r.writer[0] != ename or need_same_edge(r.writer)):
                self._wait(ename, r.writer)
        for w in writes:
            if w.writer is not None and (dma or w.writer[0] != ename):
                self._wait(ename, w.writer)
            for rd in w.readers:
                if dma or rd[0] != ename:
                    self._wait(ename, rd)
        inst = fn(e)
        self.opn[ename] += 1
        pend = self.pending.setdefault(ename, ([], []))
        if not inc:
            pend[0].extend(reads)
            pend[1].extend(writes)
            return None
        n = 16 if dma else 1
        inst.then_inc(self.sems[my_sem], n)
        self.c[my_sem] += n
        tok = (my_sem, self.c[my_sem], myop)
        for r in list(reads) + pend[0]:
            r.readers.append(tok)
        for w in list(writes) + pend[1]:
            w.writer = tok
            w.readers = []
        self.pending[ename] = ([], [])
        return tok

    def barrier_wait(self, ename, toks):
        for t in toks:
            self._wait(ename, t)

    def fence(self):
        return [(k, v) for k, v in self.c.items() if v > 0]

    def seed(self, *rs):
        f = self.fence()
        for r in rs:
            r.readers.extend(f)


def _f32(x):
    return np.ascontiguousarray(np.asarray(x), dtype=np.float32)


def _bf16np(x):
    return np.ascontiguousarray(np.asarray(x, dtype=np.float32).astype(ml_dtypes.bfloat16))


def prep_weights(i):
    w = {}
    w["wqkvT"] = _f32(np.asarray(i["L0_attn_a_W_qkv"]).T)
    w["bqkv"] = _f32(np.asarray(i["L0_attn_a_b_qkv"]).reshape(24, 128).T)
    w["wlatT"] = _f32(np.tile(np.asarray(i["L0_attn_a_W_lat"]).T, (2, 1)))
    w["blat"] = _f32(np.asarray(i["L0_attn_a_b_lat"]).reshape(64, 1))
    w["wprojT"] = _f32(np.asarray(i["L0_attn_a_W_proj"]).T)
    w["bproj"] = _f32(np.asarray(i["L0_attn_a_b_proj"]).reshape(8, 128).T)
    w["wgateT"] = _f32(np.asarray(i["L0_ffn_w_gate"]).T)
    w["W1"] = _f32(i["L0_ffn_W1"])
    w["W2J"] = _f32(np.asarray(i["L0_ffn_W2"]).reshape(8, 4, 8, 128, 1024)
                    .transpose(0, 1, 3, 2, 4))
    w["b1"] = _f32(np.asarray(i["L0_ffn_b1"]).reshape(8, 32, 128).transpose(2, 0, 1))
    w["b2"] = _f32(i["L0_ffn_b2"])
    w["winT"] = _f32(np.asarray(i["L0_attn_b_in_proj_w"]).T)
    w["convw"] = _f32(np.asarray(i["L0_attn_b_conv_w"]).reshape(16, 128, 4).transpose(1, 0, 2))
    w["convb"] = _f32(np.asarray(i["L0_attn_b_conv_b"]).reshape(16, 128).T)
    w["wxpT"] = _f32(np.asarray(i["L0_attn_b_x_proj_w"]).T)
    w["dtw"] = _f32(np.asarray(i["L0_attn_b_dt_proj_w"]).reshape(16, 128).T)
    w["dtb"] = _f32(np.asarray(i["L0_attn_b_dt_proj_b"]).reshape(16, 128).T)
    w["alog"] = _f32(np.asarray(i["L0_attn_b_A_log"]).reshape(16, 128, 16).transpose(1, 0, 2))
    w["dcol"] = _f32(np.asarray(i["L0_attn_b_D"]).reshape(16, 128).T)
    w["woutT"] = _f32(np.asarray(i["L0_attn_b_out_proj_w"]).T)
    w["ones_col16"] = _bf16np(np.ones((128, 1)))
    w["ones_col32"] = _f32(np.ones((128, 1)))
    w["ones_row32"] = _f32(np.ones((1, 128)))
    w["ones_row16"] = _bf16np(np.ones((1, 128)))
    hm = np.zeros((128, 8, 16), np.float32)
    for c in range(8):
        for p in range(128):
            hm[p, c, 2 * c + p // 64] = 1.0
    w["headmask16"] = _bf16np(hm)
    sv = np.ascontiguousarray(np.transpose(hm, (2, 1, 0)))
    w["selv16"] = _bf16np(sv)
    w["selv32"] = _f32(sv)
    sr = np.zeros((16, 16, 128), np.float32)
    for n in range(16):
        sr[n, n, :] = 1.0
    w["selrow32"] = _f32(sr)
    w["selrow16"] = _bf16np(sr)
    se = np.zeros((8, 8, 128), np.float32)
    for e_ in range(8):
        se[e_, e_, :] = 1.0
    w["sel8"] = _f32(se)
    w["eye32"] = _f32(np.eye(128))
    w["miscc"] = _f32(np.array([[1e-5]]))
    return w


WEIGHT_SHAPES = {
    "wqkvT": (1024, 3072), "bqkv": (128, 24), "wlatT": (128, 64), "blat": (64, 1),
    "wprojT": (1024, 1024), "bproj": (128, 8),
    "wgateT": (1024, 8), "W1": (8, 1024, 4096), "W2J": (8, 4, 128, 8, 1024),
    "b1": (128, 8, 32), "b2": (8, 1024),
    "winT": (1024, 4096), "convw": (128, 16, 4), "convb": (128, 16),
    "wxpT": (2048, 33), "dtw": (128, 16), "dtb": (128, 16),
    "alog": (128, 16, 16), "dcol": (128, 16), "woutT": (2048, 1024),
    "ones_col16": (128, 1), "ones_col32": (128, 1), "ones_row32": (1, 128), "ones_row16": (1, 128),
    "headmask16": (128, 8, 16), "selv16": (16, 8, 128), "selv32": (16, 8, 128),
    "selrow32": (16, 16, 128), "selrow16": (16, 16, 128), "eye32": (128, 128), "miscc": (1, 1), "sel8": (8, 8, 128),
}
BF16_WEIGHTS = {"ones_col16", "headmask16", "selv16", "ones_row16", "selrow16"}
RESIDENT = ["ones_row32", "ones_row16", "eye32", "miscc", "ones_col16", "ones_col32"]


def build_nc(S=2048, upto=4, dbg=False, strict=False, only=None):
    nc = bass.Bass("TRN2", target_bir_lowering=False, detect_race_conditions=False)
    TW = min(S, 512)
    TB = S // TW
    assert S % TW == 0 and S >= 64

    xin = nc.dram_tensor("xin", [D, S], F32, kind="ExternalInput")
    xout = nc.dram_tensor("xout", [D, S], F32, kind="ExternalOutput")
    dw = {name: nc.dram_tensor(name, list(shape), BF16 if name in BF16_WEIGHTS else F32,
                               kind="ExternalInput")
          for name, shape in WEIGHT_SHAPES.items()}
    dbg_out = {}
    if dbg:
        for n_, sh in [("d_eL", [16, 8]), ("d_sL", [16, 8]), ("d_kve", [128, 2 * 8 * 2]),
                       ("d_qe", [128, 8 * 8]), ("d_xne", [128, 8 * 8]), ("d_pe16", [128, 8]),
                       ("d_sL2", [16, 8]), ("d_den2", [16, 256]),
                       ("d_s32a", [16, 7 * 256]), ("d_e16a", [16, 7 * 256]),
                       ("d_s32z", [16, 7 * 256]), ("d_e16z", [16, 7 * 256]),
                       ("d_aoa", [128, 8 * 256]), ("d_aoz", [128, 8 * 256]),
                       ("d_dena", [16, 256]), ("d_denz", [16, 256])]:
            dbg_out[n_] = nc.dram_tensor(n_, sh, F32, kind="ExternalOutput")

    import contextlib
    with contextlib.ExitStack() as st:
        sbt = lambda n_, sh, dt=F32: st.enter_context(nc.sbuf_tensor(n_, sh, dt))
        sems = {k: st.enter_context(nc.semaphore(k)) for k in ENG + DMA_SEMS}
        b = Bld(nc, sems)
        b.strict = strict

        x32 = sbt("x32", [128, DC, S])
        mrow = sbt("mrow", [1, S], BF16)
        rrow = sbt("rrow", [1, S], BF16)
        SBUF_SHAPE = {"wgateT": [128, 8, 8], "wxpT": [128, 16, 33]}
        cst = {name: sbt("c_" + name, SBUF_SHAPE.get(name, list(WEIGHT_SHAPES[name])),
                         BF16 if name in BF16_WEIGHTS else F32) for name in RESIDENT}

        PSW = 2048
        pa_t = st.enter_context(nc.psum_tensor("pa", [128, PSW], F32))
        pb_t = st.enter_context(nc.psum_tensor("pb", [128, PSW], F32))
        BNK = 512

        R = {}

        def res(name):
            if name not in R:
                R[name] = Res(name)
            return R[name]

        rx = res("x32")
        rstat = res("stats")
        rpa = [res(f"pa{i}") for i in range(4)]
        rpb = [res(f"pb{i}") for i in range(4)]
        rcst = res("cst")

        def pa(bank, w=None, rows=(0, 128)):
            w = BNK if w is None else w
            return pa_t.ap()[rows[0]:rows[1], bank * BNK:bank * BNK + w]

        def pb(bank, w=None, rows=(0, 128)):
            w = BNK if w is None else w
            return pb_t.ap()[rows[0]:rows[1], bank * BNK:bank * BNK + w]

        b.op("sp", lambda e: e.dma_start(out=x32.ap(), in_=xin.ap().rearrange("(c p) t -> p c t", p=128)),
             writes=[rx], dma=True)
        for name in RESIDENT:
            src = dw[name].ap()
            if name in SBUF_SHAPE:
                src = src.rearrange("(c p) e -> p c e", p=128)
            b.op("sp", lambda e, s_=src, n_=name: e.dma_start(out=cst[n_].ap(), in_=s_),
                 writes=[rcst], dma=True)
        xb16 = sbt("xb16", [128, 2, TW], BF16)
        sq16 = sbt("sq16", [128, 2, TW], BF16)
        lnt = sbt("lnt", [128, 2, TW])
        rxb = [res("xb0"), res("xb1")]
        rsq = [res("sq0"), res("sq1")]
        rlnt = [res("lnt0"), res("lnt1")]

        def emit_ln_stats(eps=1e-5, mean32=None, mean32_res=None):
            for dc in range(DC):
                for tb in range(TB):
                    pp = (dc * TB + tb) % 2
                    sl = slice(tb * TW, (tb + 1) * TW)
                    b.op("dve", lambda e, pp=pp, dc=dc, sl=sl: e.tensor_copy(
                        xb16.ap()[:, pp, :], x32.ap()[:, dc, sl]),
                        reads=[rx], writes=[rxb[pp]])
                    b.op("act", lambda e, pp=pp: e.activation(sq16.ap()[:, pp, :], xb16.ap()[:, pp, :], AF.Square),
                         reads=[rxb[pp]], writes=[rsq[pp]])
                    last_mm = (dc == DC - 1)
                    b.op("pe", lambda e, sl=sl, pp=pp, dc=dc: e.matmul(
                        pa_t.ap()[0:1, sl], cst["ones_col16"].ap(), xb16.ap()[:, pp, :],
                        start=(dc == 0), stop=last_mm),
                        reads=[rxb[pp], rcst], writes=list(rpa), inc=False)
                    b.op("pe", lambda e, sl=sl, pp=pp, dc=dc: e.matmul(
                        pb_t.ap()[0:1, sl], cst["ones_col16"].ap(), sq16.ap()[:, pp, :],
                        start=(dc == 0), stop=last_mm),
                        reads=[rsq[pp], rcst], writes=list(rpb), inc=True)
            b.op("act", lambda e: e.activation(mrow.ap(), pa_t.ap()[0:1, 0:S], AF.Identity, scale=1.0 / D),
                 reads=list(rpa), writes=[rstat])
            if mean32 is not None:
                b.op("act", lambda e: e.activation(mean32, pa_t.ap()[0:1, 0:S], AF.Identity, scale=1.0 / D),
                     reads=list(rpa), writes=[mean32_res])
            b.op("act", lambda e: e.activation(rrow.ap(), pb_t.ap()[0:1, 0:S], AF.Identity, scale=1.0 / D),
                 reads=list(rpb), writes=[rstat])
            b.op("dve", lambda e: e.tensor_mul(pa_t.ap()[0:1, 0:S], mrow.ap(), mrow.ap()),
                 reads=[rstat], writes=rpa)
            b.op("dve", lambda e: e.tensor_sub(rrow.ap(), rrow.ap(), pa_t.ap()[0:1, 0:S]),
                 reads=rpa, writes=[rstat])
            b.op("act", lambda e: e.activation(rrow.ap(), rrow.ap(), AF.Ln, bias=cst["miscc"].ap()),
                 reads=[rstat, rcst], writes=[rstat])
            b.op("act", lambda e: e.activation(rrow.ap(), rrow.ap(), AF.Exp, scale=-0.5),
                 reads=[rstat], writes=[rstat])

        def emit_bcast_row(row_ap, cols, dst_bankfn, dst_res, w, src_res=None):
            c0, c1 = cols
            lhsT = cst["ones_row16"] if row_ap.dtype == BF16 else cst["ones_row32"]
            for o in range(0, w, 512):
                oe = min(o + 512, w)
                b.op("pe", lambda e, o=o, oe=oe, lhsT=lhsT: e.matmul(
                    dst_bankfn(o, oe), lhsT.ap(), row_ap[0:1, c0 + o:c0 + oe],
                    start=True, stop=True),
                    reads=[rstat if src_res is None else src_res, rcst],
                    writes=[dst_res], inc=(oe == w))

        def emit_ln_apply(dc, cols, out_ap, pp, out_res, extra32=None, extra_res=None):
            c0, c1 = cols
            w = c1 - c0
            assert w <= BNK
            emit_bcast_row(mrow.ap(), cols, lambda o, oe: pa(3, oe)[:, o:oe], rpa[3], w)
            emit_bcast_row(rrow.ap(), cols, lambda o, oe: pb(3, oe)[:, o:oe], rpb[3], w)
            b.op("dve", lambda e: e.tensor_sub(lnt.ap()[:, pp, 0:w], x32.ap()[:, dc, c0:c1], pa(3, w)),
                 reads=[rx, rpa[3]], writes=[rlnt[pp]])
            tok = b.op("dve", lambda e: e.tensor_mul(out_ap, lnt.ap()[:, pp, 0:w], pb(3, w)),
                       reads=[rlnt[pp], rpb[3]], writes=[out_res])
            if extra32 is not None:
                tok = b.op("dve", lambda e: e.tensor_mul(extra32, lnt.ap()[:, pp, 0:w], pb(3, w)),
                           reads=[rlnt[pp], rpb[3]], writes=[extra_res])
            return tok

        enabled = set(only) if only is not None else set(range(1, upto + 1))
        if 1 in enabled:
            with contextlib.ExitStack() as ph:
                psb = lambda n_, sh, dt=F32: ph.enter_context(nc.sbuf_tensor(n_, sh, dt))
                b.seed(rcst)
                acst = {}
                for name in ["bqkv", "blat", "bproj", "headmask16", "selv16", "selv32"]:
                    acst[name] = psb("c_" + name, list(WEIGHT_SHAPES[name]),
                                     BF16 if name in BF16_WEIGHTS else F32)
                    b.op("sp", lambda e, n_=name: e.dma_start(out=acst[n_].ap(), in_=dw[n_].ap()),
                         writes=[rcst], dma=True)
                wlatT16 = psb("wlatT16", [128, 64], BF16)
                b.op("pool", lambda e: e.dma_start(out=wlatT16.ap(), in_=dw["wlatT"].ap()),
                     writes=[rcst], dma=True)

                emit_ln_stats()

                PW = min(S, 256)
                NQ = S // PW
                kv16 = psb("kv16", [128, 2, DC, S], BF16)
                xnS = psb("a_xnS", [128, DC, S], BF16)
                wbuf = psb("a_wbuf", [128, 2, DC, 128], BF16)
                ao16 = psb("a_ao16", [128, 1, DC, PW], BF16)
                acc = psb("a_acc", [128, 2, PW])
                s32 = psb("a_s32", [16, WIN, PW])
                e16 = psb("a_e16", [16, WIN, PW], BF16)
                mx16 = psb("a_mx", [16, PW])
                den = psb("a_den", [16, PW])
                prod16 = psb("a_prod", [128, PW], BF16)
                q16 = psb("a_q16", [128, PW], BF16)
                eL = psb("a_eL", [16, 8])
                qe16 = psb("a_qe16", [128, DC, 8], BF16)
                xne = psb("a_xne", [128, DC, 8], BF16)
                pe16 = psb("a_pe16", [128, 8], BF16)
                sL = psb("a_sL", [16, 8])
                kve = psb("a_kve", [128, 2, DC, 2], BF16)

                rkv = res("kv16"); rxn = res("a_xnS")
                rwb = [res("a_wb0"), res("a_wb1")]
                rao = [res("a_ao0"), res("a_ao1")]
                racc = [res("a_acc0"), res("a_acc1")]
                rs32 = res("a_s32"); re16 = res("a_e16"); rmx = res("a_mx"); rden = res("a_den")
                rprod = res("a_prod"); rq = res("a_q16"); reL = res("a_eL"); rqe = res("a_qe")
                rxne = res("a_xne"); rpe16 = res("a_pe16"); rsL = res("a_sL"); rkve = res("a_kve")

                wqkv_r = dw["wqkvT"].ap().rearrange("(c p) o -> p c o", p=128)
                wproj_r = dw["wprojT"].ap().rearrange("(c p) o -> p c o", p=128)

                def load_w2(buf, src_r, oc0):
                    return b.op("pool", lambda e: e.dma_start(
                        out=wbuf.ap()[:, buf, :, :],
                        in_=src_r[:, :, oc0 * 128:(oc0 + 1) * 128]),
                        writes=[rwb[buf]], dma=True)

                def qkv_group(oc, buf, j, rhs_ap, rhs_res, n, dst_ap, dst_res):
                    for dc in range(DC):
                        b.op("pe", lambda e, dc=dc: e.matmul(
                            dst_ap, wbuf.ap()[:, buf, dc, :],
                            rhs_ap[:, dc, 0:n], start=(dc == 0), stop=(dc == DC - 1)),
                            reads=[rwb[buf], rhs_res, rcst], writes=[dst_res], inc=(dc == DC - 1))

                b.op("dve", lambda e: e.memset(pe16.ap(), 0.0), writes=[rpe16])
                b.op("dve", lambda e: e.memset(xne.ap(), 0.0), writes=[rxne])
                for dc in range(DC):
                    b.op("pe", lambda e: e.matmul(pa(3, 3), cst["ones_row16"].ap(), mrow.ap()[0:1, 0:3],
                                                  start=True, stop=True), reads=[rstat, rcst], writes=[rpa[3]], inc=False)
                    b.op("pe", lambda e: e.matmul(pa(3, 8)[:, 4:7], cst["ones_row16"].ap(), mrow.ap()[0:1, S - 3:S],
                                                  start=True, stop=True), reads=[rstat, rcst], writes=[rpa[3]])
                    b.op("pe", lambda e: e.matmul(pb(3, 3), cst["ones_row16"].ap(), rrow.ap()[0:1, 0:3],
                                                  start=True, stop=True), reads=[rstat, rcst], writes=[rpb[3]], inc=False)
                    b.op("pe", lambda e: e.matmul(pb(3, 8)[:, 4:7], cst["ones_row16"].ap(), rrow.ap()[0:1, S - 3:S],
                                                  start=True, stop=True), reads=[rstat, rcst], writes=[rpb[3]])
                    b.op("dve", lambda e, dc=dc: e.tensor_sub(lnt.ap()[:, 0, 0:3], x32.ap()[:, dc, 0:3], pa(3, 3)),
                         reads=[rx, rpa[3]], writes=[rlnt[0]])
                    b.op("dve", lambda e, dc=dc: e.tensor_sub(lnt.ap()[:, 0, 4:7], x32.ap()[:, dc, S - 3:S],
                                                              pa(3, 8)[:, 4:7]),
                         reads=[rx, rpa[3]], writes=[rlnt[0]])
                    b.op("dve", lambda e, dc=dc: e.tensor_mul(xne.ap()[:, dc, 0:3], lnt.ap()[:, 0, 0:3], pb(3, 3)),
                         reads=[rlnt[0], rpb[3]], writes=[rxne])
                    b.op("dve", lambda e, dc=dc: e.tensor_mul(xne.ap()[:, dc, 4:7], lnt.ap()[:, 0, 4:7],
                                                              pb(3, 8)[:, 4:7]),
                         reads=[rlnt[0], rpb[3]], writes=[rxne])

                for g in range(16):
                    buf = g % 2
                    load_w2(buf, wqkv_r, 8 + g)
                    for j in range(1):
                        oc = 8 + g
                        kvi = (oc - 8) // 8
                        dcv = (oc - 8) % 8
                        for dc in range(DC):
                            b.op("pe", lambda e, dc=dc, buf=buf, j=j: e.matmul(
                                pa(2, 8), wbuf.ap()[:, buf, dc, j * 128:(j + 1) * 128], xne.ap()[:, dc, :],
                                start=(dc == 0), stop=(dc == DC - 1)),
                                reads=[rwb[buf], rxne, rcst], writes=[rpa[2]], inc=(dc == DC - 1))
                        b.op("act", lambda e, kvi=kvi, dcv=dcv, oc=oc: e.activation(
                            kve.ap()[:, kvi, dcv, 0:1], pa(2, 8)[:, 0:1], AF.Identity,
                            bias=acst["bqkv"].ap()[:, oc:oc + 1]),
                            reads=[rpa[2], rcst], writes=[rkve])
                        b.op("act", lambda e, kvi=kvi, dcv=dcv, oc=oc: e.activation(
                            kve.ap()[:, kvi, dcv, 1:2], pa(2, 8)[:, 6:7], AF.Identity,
                            bias=acst["bqkv"].ap()[:, oc:oc + 1]),
                            reads=[rpa[2], rcst], writes=[rkve])
                for kvi in range(2):
                    for dcv in range(DC):
                        for hh in range(2):
                            rows = slice(hh * 64, hh * 64 + 64)
                            b.op("pe", lambda e, kvi=kvi, dcv=dcv, rows=rows: e.matmul(
                                pa(2, 2, (rows.start, rows.start + 64)), wlatT16.ap()[rows, :],
                                kve.ap()[:, kvi, dcv, :][rows, :], start=True, stop=True),
                                reads=[rkve, rcst], writes=[rpa[2]])
                            b.op("act", lambda e, kvi=kvi, dcv=dcv, rows=rows: e.activation(
                                kve.ap()[:, kvi, dcv, :][rows, :],
                                pa(2, 2, (rows.start, rows.start + 64)), AF.Identity,
                                bias=acst["blat"].ap()),
                                reads=[rpa[2], rcst], writes=[rkve])

                for g in range(8):
                    buf = g % 2
                    load_w2(buf, wqkv_r, g)
                    for j in range(1):
                        c = g
                        qkv_group(c, buf, j, xne.ap(), rxne, 8, pa(2, 8), rpa[2])
                        b.op("act", lambda e, c=c: e.activation(qe16.ap()[:, c, :], pa(2, 8), AF.Identity,
                                                                bias=acst["bqkv"].ap()[:, c:c + 1]),
                             reads=[rpa[2], rcst], writes=[rqe])
                for c in range(DC):
                    b.op("dve", lambda e, c=c: e.tensor_mul(pe16.ap()[:, 0:3], qe16.ap()[:, c, 0:3],
                                                            kve.ap()[:, 0, c, 1:2].broadcast_to([128, 3])),
                         reads=[rqe, rkve], writes=[rpe16])
                    b.op("dve", lambda e, c=c: e.tensor_mul(pe16.ap()[:, 4:7], qe16.ap()[:, c, 4:7],
                                                            kve.ap()[:, 0, c, 0:1].broadcast_to([128, 3])),
                         reads=[rqe, rkve], writes=[rpe16])
                    b.op("pe", lambda e, c=c: e.matmul(pb(3, 8, (0, 16)), acst["headmask16"].ap()[:, c, :],
                                                       pe16.ap(), start=(c == 0), stop=(c == DC - 1)),
                        reads=[rpe16, rcst], writes=[rpb[3]], inc=True)
                b.op("act", lambda e: e.activation(eL.ap(), pb(3, 8, (0, 16)), AF.Identity, scale=1.0 / 8.0),
                     reads=[rpb[3]], writes=[reL])
                if dbg:
                    b.op("sp", lambda e: e.dma_start(out=dbg_out["d_eL"].ap(), in_=eL.ap()),
                         reads=[reL], dma=True)
                    b.op("pool", lambda e: e.dma_start(out=dbg_out["d_qe"].ap(),
                         in_=qe16.ap().rearrange("p a b -> p (a b)")), reads=[rqe], dma=True)
                    b.op("pool", lambda e: e.dma_start(out=dbg_out["d_xne"].ap(),
                         in_=xne.ap().rearrange("p a b -> p (a b)")), reads=[rxne], dma=True)
                    b.op("pool", lambda e: e.dma_start(out=dbg_out["d_pe16"].ap(),
                         in_=pe16.ap()), reads=[rpe16], dma=True)
                    b.op("pool", lambda e: e.dma_start(
                        out=dbg_out["d_kve"].ap(),
                        in_=kve.ap().rearrange("p a b c -> p (a b c)")), reads=[rkve], dma=True)

                for tb in range(TB):
                    T0 = tb * TW
                    for dc in range(DC):
                        emit_ln_apply(dc, (T0, T0 + TW), xnS.ap()[:, dc, T0:T0 + TW], dc % 2, rxn)
                for g in range(16):
                    buf = g % 2
                    load_w2(buf, wqkv_r, 8 + g)
                    for j in range(1):
                        oc = 8 + g
                        kvi = (oc - 8) // 8
                        dcv = (oc - 8) % 8
                        for tb in range(TB):
                            T0 = tb * TW
                            sl = slice(T0, T0 + TW)
                            pslot = (g + tb) % 3
                            qkv_group(oc, buf, j, xnS.ap()[:, :, sl], rxn, TW, pb(pslot, TW), rpb[pslot])
                            b.op("act", lambda e, kvi=kvi, dcv=dcv, sl=sl, oc=oc, pslot=pslot: e.activation(
                                kv16.ap()[:, kvi, dcv, sl], pb(pslot, TW), AF.Identity,
                                bias=acst["bqkv"].ap()[:, oc:oc + 1]),
                                reads=[rpb[pslot], rcst], writes=[rkv])
                for kvi in range(2):
                    for dcv in range(DC):
                        for hh in range(2):
                            rows = slice(hh * 64, hh * 64 + 64)
                            for tb in range(TB):
                                sl = slice(tb * TW, (tb + 1) * TW)
                                pslot = tb % 2
                                b.op("pe", lambda e, kvi=kvi, dcv=dcv, rows=rows, sl=sl, pslot=pslot: e.matmul(
                                    pa(pslot, TW, (rows.start, rows.start + 64)), wlatT16.ap()[rows, :],
                                    kv16.ap()[:, kvi, dcv, sl][rows, :], start=True, stop=True),
                                    reads=[rkv, rcst], writes=[rpa[pslot]])
                                b.op("act", lambda e, kvi=kvi, dcv=dcv, rows=rows, sl=sl, pslot=pslot: e.activation(
                                    kv16.ap()[:, kvi, dcv, sl][rows, :],
                                    pa(pslot, TW, (rows.start, rows.start + 64)), AF.Identity,
                                    bias=acst["blat"].ap()),
                                    reads=[rpa[pslot], rcst], writes=[rkv])

                DLT = list(range(-3, 4))
                sc_bank = [(pa_t, rpa, 0), (pa_t, rpa, 1), (pa_t, rpa, 2), (pa_t, rpa, 3),
                           (pb_t, rpb, 0), (pb_t, rpb, 1), (pb_t, rpb, 2)]

                def sc_ap(dd, a0, a1):
                    t_, _, k_ = sc_bank[dd]
                    return t_.ap()[0:16, k_ * BNK + a0:k_ * BNK + a1]

                for tq in range(NQ):
                    T0 = tq * PW
                    sl = slice(T0, T0 + PW)
                    app = 0
                    for g in range(8):
                        buf = g % 2
                        load_w2(buf, wqkv_r, g)
                        for j in range(1):
                            c = g
                            qkv_group(c, buf, j, xnS.ap()[:, :, sl], rxn, PW, pb(3, PW), rpb[3])
                            b.op("act", lambda e, c=c: e.activation(q16.ap(), pb(3, PW), AF.Identity,
                                                                    bias=acst["bqkv"].ap()[:, c:c + 1]),
                                 reads=[rpb[3], rcst], writes=[rq])
                            for dd, dlt in enumerate(DLT):
                                lo = max(0, T0 + dlt)
                                hi = min(S, T0 + PW + dlt)
                                if hi <= lo:
                                    continue
                                a0, a1 = lo - dlt - T0, hi - dlt - T0
                                b.op("dve", lambda e, a0=a0, a1=a1, lo=lo, hi=hi, c=c: e.tensor_mul(
                                    prod16.ap()[:, a0:a1], q16.ap()[:, a0:a1], kv16.ap()[:, 0, c, lo:hi]),
                                    reads=[rq, rkv], writes=[rprod])
                                t_, rl_, k_ = sc_bank[dd]
                                b.op("pe", lambda e, t_=t_, k_=k_, a0=a0, a1=a1, c=c: e.matmul(
                                    t_.ap()[0:16, k_ * BNK + a0:k_ * BNK + a1], acst["headmask16"].ap()[:, c, :],
                                    prod16.ap()[:, a0:a1], start=(c == 0), stop=(c == DC - 1)),
                                    reads=[rprod, rcst], writes=[rl_[k_]], inc=True)
                    for dd in range(WIN):
                        dlt = dd - 3
                        v0 = max(0, -(T0 + dlt))
                        v1 = PW - max(0, T0 + PW + dlt - S)
                        t_, rl_, k_ = sc_bank[dd]
                        b.op("act", lambda e, dd=dd, v0=v0, v1=v1: e.activation(
                            s32.ap()[:, dd, v0:v1], sc_ap(dd, v0, v1), AF.Identity, scale=1.0 / 8.0),
                             reads=[rl_[k_]], writes=[rs32])
                    if tq == 0:
                        for dd in range(3):
                            ncols = 3 - dd
                            b.op("dve", lambda e, dd=dd, ncols=ncols: e.tensor_copy(
                                s32.ap()[:, dd, 0:ncols], eL.ap()[:, 0:ncols]),
                                reads=[reL], writes=[rs32])
                    if tq == NQ - 1:
                        for dd in range(4, 7):
                            ncols = dd - 3
                            b.op("dve", lambda e, dd=dd, ncols=ncols: e.tensor_copy(
                                s32.ap()[:, dd, PW - ncols:PW], eL.ap()[:, 7 - ncols:7]),
                                reads=[reL], writes=[rs32])
                    b.op("dve", lambda e: e.tensor_max(mx16.ap(), s32.ap()[:, 0, :], s32.ap()[:, 1, :]),
                         reads=[rs32], writes=[rmx])
                    for dd in range(2, WIN):
                        b.op("dve", lambda e, dd=dd: e.tensor_max(mx16.ap(), mx16.ap(), s32.ap()[:, dd, :]),
                             reads=[rs32, rmx], writes=[rmx])
                    for dd in range(WIN):
                        b.op("dve", lambda e, dd=dd: e.tensor_sub(s32.ap()[:, dd, :], s32.ap()[:, dd, :], mx16.ap()),
                             reads=[rmx, rs32], writes=[rs32])
                    for dd in range(WIN):
                        b.op("act", lambda e, dd=dd: e.activation(e16.ap()[:, dd, :], s32.ap()[:, dd, :], AF.Exp),
                             reads=[rs32], writes=[re16])
                    b.op("dve", lambda e: e.tensor_add(den.ap(), e16.ap()[:, 0, :], e16.ap()[:, 1, :]),
                         reads=[re16], writes=[rden])
                    for dd in range(2, WIN):
                        b.op("dve", lambda e, dd=dd: e.tensor_add(den.ap(), den.ap(), e16.ap()[:, dd, :]),
                             reads=[re16, rden], writes=[rden])
                    b.op("dve", lambda e: e.reciprocal(den.ap(), den.ap()), reads=[rden], writes=[rden])
                    if tq == 0:
                        b.op("dve", lambda e: e.tensor_add(sL.ap()[:, 3:4], e16.ap()[:, 0, 0:1], e16.ap()[:, 1, 0:1]),
                             reads=[re16], writes=[rsL])
                        b.op("dve", lambda e: e.tensor_add(sL.ap()[:, 0:1], sL.ap()[:, 3:4], e16.ap()[:, 2, 0:1]),
                             reads=[re16, rsL], writes=[rsL])
                        b.op("dve", lambda e: e.tensor_add(sL.ap()[:, 1:2], e16.ap()[:, 0, 1:2], e16.ap()[:, 1, 1:2]),
                             reads=[re16], writes=[rsL])
                        b.op("dve", lambda e: e.tensor_copy(sL.ap()[:, 2:3], e16.ap()[:, 0, 2:3]),
                             reads=[re16], writes=[rsL])
                        if dbg:
                            b.op("sp", lambda e: e.dma_start(out=dbg_out["d_sL2"].ap(), in_=sL.ap()),
                                 reads=[rsL], dma=True)
                            b.op("sp", lambda e: e.dma_start(out=dbg_out["d_den2"].ap(), in_=den.ap()),
                                 reads=[rden], dma=True)
                    if tq == NQ - 1:
                        b.op("dve", lambda e: e.tensor_add(sL.ap()[:, 7:8], e16.ap()[:, 6, PW - 1:PW],
                                                           e16.ap()[:, 5, PW - 1:PW]),
                             reads=[re16], writes=[rsL])
                        b.op("dve", lambda e: e.tensor_add(sL.ap()[:, 6:7], sL.ap()[:, 7:8], e16.ap()[:, 4, PW - 1:PW]),
                             reads=[re16, rsL], writes=[rsL])
                        b.op("dve", lambda e: e.tensor_add(sL.ap()[:, 5:6], e16.ap()[:, 6, PW - 2:PW - 1],
                                                           e16.ap()[:, 5, PW - 2:PW - 1]),
                             reads=[re16], writes=[rsL])
                        b.op("dve", lambda e: e.tensor_copy(sL.ap()[:, 4:5], e16.ap()[:, 6, PW - 3:PW - 2]),
                             reads=[re16], writes=[rsL])

                    if dbg and tq in (0, NQ - 1):
                        sfx = "a" if tq == 0 else "z"
                        b.op("sp", lambda e, sfx=sfx: e.dma_start(
                            out=dbg_out["d_s32" + sfx].ap(), in_=s32.ap().rearrange("h w t -> h (w t)")),
                            reads=[rs32], dma=True)
                        b.op("pool", lambda e, sfx=sfx: e.dma_start(
                            out=dbg_out["d_e16" + sfx].ap(), in_=e16.ap().rearrange("h w t -> h (w t)")),
                            reads=[re16], dma=True)
                        b.op("sp", lambda e, sfx=sfx: e.dma_start(
                            out=dbg_out["d_den" + sfx].ap(), in_=den.ap()), reads=[rden], dma=True)
                        if tq == NQ - 1:
                            b.op("sp", lambda e: e.dma_start(out=dbg_out["d_sL"].ap(), in_=sL.ap()),
                                 reads=[rsL], dma=True)
                    for c in range(DC):
                        cp = c % 2
                        first = True
                        for dd, dlt in enumerate(DLT):
                            lo = max(0, T0 + dlt)
                            hi = min(S, T0 + PW + dlt)
                            if hi <= lo:
                                continue
                            a0, a1 = lo - dlt - T0, hi - dlt - T0
                            pslot = dd % 2
                            b.op("pe", lambda e, pslot=pslot, c=c, dd=dd, a0=a0, a1=a1: e.matmul(
                                pa(pslot, a1 - a0), acst["selv16"].ap()[:, c, :], e16.ap()[:, dd, a0:a1],
                                start=True, stop=True),
                                reads=[re16, rcst], writes=[rpa[pslot]])
                            b.op("dve", lambda e, pslot=pslot, a0=a0, a1=a1, c=c, lo=lo, hi=hi: e.tensor_mul(
                                prod16.ap()[:, a0:a1], pa(pslot, a1 - a0), kv16.ap()[:, 1, c, lo:hi]),
                                reads=[rpa[pslot], rkv], writes=[rprod])
                            if first:
                                if a0 > 0:
                                    b.op("pool", lambda e, cp=cp, a0=a0: e.memset(acc.ap()[:, cp, 0:a0], 0.0),
                                         writes=[racc[cp]])
                                if a1 < PW:
                                    b.op("pool", lambda e, cp=cp, a1=a1: e.memset(acc.ap()[:, cp, a1:PW], 0.0),
                                         writes=[racc[cp]])
                                b.op("pool", lambda e, cp=cp, a0=a0, a1=a1: e.tensor_copy(
                                    acc.ap()[:, cp, a0:a1], prod16.ap()[:, a0:a1]),
                                    reads=[rprod], writes=[racc[cp]])
                                first = False
                            else:
                                b.op("pool", lambda e, cp=cp, a0=a0, a1=a1: e.tensor_add(
                                    acc.ap()[:, cp, a0:a1], acc.ap()[:, cp, a0:a1], prod16.ap()[:, a0:a1]),
                                    reads=[rprod, racc[cp]], writes=[racc[cp]])
                        for cond, ec0, col0, vcol in [(tq == 0, 0, 0, 1), (tq == NQ - 1, 4, PW - 3, 0)]:
                            if not cond:
                                continue
                            b.op("pe", lambda e, c=c, ec0=ec0: e.matmul(
                                pa(2, 3), acst["selv32"].ap()[:, c, :], sL.ap()[:, ec0:ec0 + 3],
                                start=True, stop=True),
                                reads=[rsL, rcst], writes=[rpa[2]])
                            b.op("dve", lambda e, c=c, vcol=vcol: e.tensor_mul(
                                lnt.ap()[:, 0, 0:3], pa(2, 3),
                                kve.ap()[:, 1, c, vcol:vcol + 1].broadcast_to([128, 3])),
                                reads=[rpa[2], rkve], writes=[rlnt[0]])
                            b.op("pool", lambda e, c=c, cp=cp, col0=col0: e.tensor_add(
                                acc.ap()[:, cp, col0:col0 + 3], acc.ap()[:, cp, col0:col0 + 3],
                                lnt.ap()[:, 0, 0:3]),
                                reads=[rlnt[0], racc[cp]], writes=[racc[cp]])
                        b.op("pe", lambda e, c=c: e.matmul(pa(3, PW), acst["selv32"].ap()[:, c, :], den.ap(),
                                                           start=True, stop=True),
                             reads=[rden, rcst], writes=[rpa[3]])
                        b.op("dve", lambda e, c=c, cp=cp, app=app: e.tensor_mul(
                            ao16.ap()[:, app, c, :], acc.ap()[:, cp, :], pa(3, PW)),
                             reads=[racc[cp], rpa[3]], writes=[rao[app]])

                    if dbg and tq in (0, NQ - 1):
                        sfx = "a" if tq == 0 else "z"
                        b.op("pool", lambda e, sfx=sfx, app=app: e.dma_start(
                            out=dbg_out["d_ao" + sfx].ap(),
                            in_=ao16.ap()[:, app].rearrange("p c t -> p (c t)")),
                            reads=[rao[app]], dma=True)
                    for g in range(8):
                        buf = g % 2
                        load_w2(buf, wproj_r, g)
                        for j in range(1):
                            oc = g
                            pslot = oc % 2
                            for c in range(DC):
                                b.op("pe", lambda e, buf=buf, c=c, app=app, pslot=pslot: e.matmul(
                                    pb(pslot, PW), wbuf.ap()[:, buf, c, :],
                                    ao16.ap()[:, app, c, :], start=(c == 0), stop=(c == DC - 1)),
                                    reads=[rwb[buf], rao[app], rcst], writes=[rpb[pslot]], inc=(c == DC - 1))
                            b.op("act", lambda e, oc=oc, pslot=pslot: e.activation(
                                lnt.ap()[:, oc % 2, 0:PW], pb(pslot, PW), AF.Identity,
                                bias=acst["bproj"].ap()[:, oc:oc + 1]),
                                reads=[rpb[pslot], rcst], writes=[rlnt[oc % 2]])
                            b.op("dve", lambda e, oc=oc, sl=sl: e.tensor_add(
                                x32.ap()[:, oc, sl], x32.ap()[:, oc, sl], lnt.ap()[:, oc % 2, 0:PW]),
                                reads=[rlnt[oc % 2], rx], writes=[rx])


        def emit_ffn(tag):
            with contextlib.ExitStack() as ph:
                psb = lambda n_, sh, dt=F32: ph.enter_context(nc.sbuf_tensor(tag + n_, sh, dt))
                b.seed(rcst)
                fcst = {}
                for name, cdt in [("wgateT", F32), ("b1", F32), ("b2", BF16), ("sel8", BF16)]:
                    shp = SBUF_SHAPE.get(name, list(WEIGHT_SHAPES[name]))
                    fcst[name] = psb("c_" + name, shp, cdt)
                    src = dw[name].ap()
                    if name in SBUF_SHAPE:
                        src = src.rearrange("(c p) e -> p c e", p=128)
                    eng_ = "pool" if cdt == BF16 else "sp"
                    b.op(eng_, lambda e, s_=src, n_=name: e.dma_start(out=fcst[n_].ap(), in_=s_),
                         writes=[rcst], dma=True)

                mrow32 = psb("mrow32", [1, S])
                rm32 = res(tag + "m32")
                rxnf = res(tag + "xn16f")
                rcw = res(tag + "cw16")
                b.seed(rxnf, rcw, rm32, res(tag + "g32"), res(tag + "gt"), res(tag + "gr"),
                       res(tag + "ws8"))
                emit_ln_stats(mean32=mrow32.ap(), mean32_res=rm32)

                JS = 1024
                NJS = FH // JS
                JC = JS // 128
                xn16f = psb("xn16f", [128, DC, S], BF16)
                cw16 = psb("cw16", [8, S], BF16)
                rbcw = res(tag + "bcw16")
                rwb = [res(tag + "wb0"), res(tag + "wb1")]
                rh = res(tag + "h16")
                rtmp = [res(tag + "tmp0"), res(tag + "tmp1")]

                for tb in range(TB):
                    T0 = tb * TW
                    for dc in range(DC):
                        emit_ln_apply(dc, (T0, T0 + TW), xn16f.ap()[:, dc, T0:T0 + TW], dc % 2, rxnf)
                with contextlib.ExitStack() as gph:
                    g32 = gph.enter_context(nc.sbuf_tensor(tag + "g32", [8, S], F32))
                    gt = gph.enter_context(nc.sbuf_tensor(tag + "gt", [128, 12, 8], F32))
                    gr = gph.enter_context(nc.sbuf_tensor(tag + "gr", [128, 6], F32))
                    ws8 = gph.enter_context(nc.sbuf_tensor(tag + "ws8", [8, 2], F32))
                    wsr = gph.enter_context(nc.sbuf_tensor(tag + "wsr", [1, 8], F32))
                    rg = res(tag + "g32")
                    rgt = res(tag + "gt")
                    rgr = res(tag + "gr")
                    rws = res(tag + "ws8")
                    for dc in range(DC):
                        b.op("pe", lambda e, dc=dc: e.matmul(
                            pb(3, 1, (0, 8)), fcst["wgateT"].ap()[:, dc, :], cst["ones_col32"].ap(),
                            start=True, stop=True),
                            reads=[rcst], writes=[rpb[3]])
                        if dc == 0:
                            b.op("dve", lambda e: e.tensor_copy(ws8.ap()[:, 0:1], pb(3, 1, (0, 8))),
                                 reads=[rpb[3]], writes=[rws])
                        else:
                            b.op("dve", lambda e: e.tensor_add(ws8.ap()[:, 0:1], ws8.ap()[:, 0:1],
                                                               pb(3, 1, (0, 8))),
                                 reads=[rpb[3], rws], writes=[rws])
                    b.op("pe", lambda e: e.transpose(pb(3, 8, (0, 1)), ws8.ap()[:, 0:1],
                                                     cst["eye32"].ap()[0:8, 0:8]),
                         reads=[rws, rcst], writes=[rpb[3]])
                    b.op("dve", lambda e: e.tensor_copy(wsr.ap(), pb(3, 8, (0, 1))),
                         reads=[rpb[3]], writes=[rws])
                    for tb in range(TB):
                        T0 = tb * TW
                        gb = tb % 2
                        for dc in range(DC):
                            b.op("pe", lambda e, dc=dc, tb=tb, gb=gb: e.matmul(
                                pa(gb, TW, (0, 8)), fcst["wgateT"].ap()[:, dc, :],
                                x32.ap()[:, dc, tb * TW:(tb + 1) * TW], start=True, stop=True),
                                reads=[rx, rcst], writes=[rpa[gb]])
                            if dc == 0:
                                b.op("dve", lambda e, tb=tb, gb=gb: e.tensor_copy(
                                    g32.ap()[:, tb * TW:(tb + 1) * TW], pa(gb, TW, (0, 8))),
                                    reads=[rpa[gb]], writes=[rg])
                            else:
                                b.op("dve", lambda e, tb=tb, gb=gb: e.tensor_add(
                                    g32.ap()[:, tb * TW:(tb + 1) * TW], g32.ap()[:, tb * TW:(tb + 1) * TW],
                                    pa(gb, TW, (0, 8))),
                                    reads=[rpa[gb], rg], writes=[rg])
                        b.op("pe", lambda e, tb=tb, gb=gb: e.matmul(
                            pb(gb, TW, (0, 8)), wsr.ap(), mrow32.ap()[0:1, tb * TW:(tb + 1) * TW],
                            start=True, stop=True),
                            reads=[rws, rm32], writes=[rpb[gb]])
                        b.op("dve", lambda e, tb=tb, gb=gb: e.tensor_sub(
                            g32.ap()[:, tb * TW:(tb + 1) * TW], g32.ap()[:, tb * TW:(tb + 1) * TW],
                            pb(gb, TW, (0, 8))),
                            reads=[rpb[gb], rg], writes=[rg])
                        b.op("pe", lambda e, tb=tb, gb=gb: e.matmul(
                            pb(gb, TW, (0, 8)), cst["ones_row16"].ap()[0:1, 0:8],
                            rrow.ap()[0:1, tb * TW:(tb + 1) * TW], start=True, stop=True),
                            reads=[rstat, rcst], writes=[rpb[gb]])
                        b.op("dve", lambda e, tb=tb, gb=gb: e.tensor_mul(
                            g32.ap()[:, tb * TW:(tb + 1) * TW], g32.ap()[:, tb * TW:(tb + 1) * TW],
                            pb(gb, TW, (0, 8))),
                            reads=[rpb[gb], rg], writes=[rg])

                    NT = S // 128
                    for t_ in range(NT):
                        ti = t_ % 8
                        cols = slice(t_ * 128, (t_ + 1) * 128)
                        b.op("pe", lambda e, cols=cols: e.transpose(
                            pb(3, 8), g32.ap()[0:8, cols], cst["eye32"].ap()[0:8, 0:8]),
                            reads=[rg, rcst], writes=[rpb[3]])
                        b.op("dve", lambda e, ti=ti: e.tensor_copy(gt.ap()[:, ti, :], pb(3, 8)),
                             reads=[rpb[3]], writes=[rgt])
                        b.op("dve", lambda e, ti=ti: e.reduce_max(gr.ap()[:, 0:1], gt.ap()[:, ti, :], AX.X),
                             reads=[rgt], writes=[rgr])
                        b.op("dve", lambda e, ti=ti: e.tensor_scalar(
                            gt.ap()[:, ti + 1 if False else 9, :], gt.ap()[:, ti, :], gr.ap()[:, 0:1], None,
                            ALU.subtract),
                            reads=[rgt, rgr], writes=[rgt])
                        b.op("act", lambda e: e.activation(gt.ap()[:, 10, :], gt.ap()[:, 9, :], AF.Exp),
                             reads=[rgt], writes=[rgt])
                        b.op("dve", lambda e: e.reduce_sum(gr.ap()[:, 1:2], gt.ap()[:, 10, :], AX.X),
                             reads=[rgt], writes=[rgr])
                        b.op("dve", lambda e: e.reciprocal(gr.ap()[:, 2:3], gr.ap()[:, 1:2]),
                             reads=[rgr], writes=[rgr])
                        b.op("dve", lambda e, ti=ti: e.tensor_scalar(
                            gt.ap()[:, 11, :], gt.ap()[:, ti, :], gr.ap()[:, 0:1], None, ALU.is_equal),
                            reads=[rgt, rgr], writes=[rgt])
                        b.op("dve", lambda e, ti=ti: e.scalar_tensor_tensor(
                            gt.ap()[:, 9, :], gt.ap()[:, 11, :], -1e30, gt.ap()[:, ti, :],
                            ALU.mult, ALU.add),
                            reads=[rgt], writes=[rgt])
                        b.op("dve", lambda e: e.reduce_max(gr.ap()[:, 3:4], gt.ap()[:, 9, :], AX.X),
                             reads=[rgt], writes=[rgr])
                        b.op("dve", lambda e: e.tensor_scalar(
                            gt.ap()[:, 9, :], gt.ap()[:, 9, :], gr.ap()[:, 3:4], None, ALU.is_equal),
                            reads=[rgt, rgr], writes=[rgt])
                        b.op("dve", lambda e: e.tensor_add(gt.ap()[:, 9, :], gt.ap()[:, 9, :], gt.ap()[:, 11, :]),
                             reads=[rgt], writes=[rgt])
                        b.op("dve", lambda e: e.tensor_scalar(
                            gt.ap()[:, 10, :], gt.ap()[:, 10, :], gr.ap()[:, 2:3], None, ALU.mult),
                            reads=[rgt, rgr], writes=[rgt])
                        b.op("dve", lambda e: e.tensor_mul(gt.ap()[:, 10, :], gt.ap()[:, 10, :], gt.ap()[:, 9, :]),
                             reads=[rgt], writes=[rgt])
                        b.op("pe", lambda e: e.transpose(
                            pb(3, 128, (0, 8)), gt.ap()[:, 10, :], cst["eye32"].ap()),
                            reads=[rgt, rcst], writes=[rpb[3]])
                        b.op("dve", lambda e, cols=cols: e.tensor_copy(cw16.ap()[:, cols], pb(3, 128, (0, 8))),
                             reads=[rpb[3]], writes=[rcw])

                for dc in range(DC):
                    for tb in range(TB):
                        gb = (dc * TB + tb) % 2
                        b.op("pe", lambda e, dc=dc, tb=tb, gb=gb: e.matmul(
                            pa(gb, TW), fcst["b2"].ap()[:, dc * 128:(dc + 1) * 128],
                            cw16.ap()[:, tb * TW:(tb + 1) * TW], start=True, stop=True),
                            reads=[rcw, rcst], writes=[rpa[gb]])
                        b.op("dve", lambda e, dc=dc, tb=tb, gb=gb: e.tensor_add(
                            x32.ap()[:, dc, tb * TW:(tb + 1) * TW], x32.ap()[:, dc, tb * TW:(tb + 1) * TW],
                            pa(gb, TW)),
                            reads=[rpa[gb], rx], writes=[rx])

                bcw16 = psb("bcw16", [128, S], BF16)
                wbig = psb("wbig", [128, 2, 8192], BF16)
                h16 = psb("h16", [128, JC, S], BF16)
                tmp32 = psb("tmp32", [128, 2, 512])
                b.seed(rbcw, rwb[0], rwb[1], rh, rtmp[0], rtmp[1])
                w1_r = dw["W1"].ap().rearrange("e (c p) j -> e p c j", p=128)
                w2_r = dw["W2J"].ap()
                wb_tok = [None, None]
                for ex in range(NE):
                    for tb in range(TB):
                        gb = tb % 2
                        b.op("pe", lambda e, ex=ex, tb=tb, gb=gb: e.matmul(
                            pb(gb, TW), fcst["sel8"].ap()[:, ex, :], cw16.ap()[:, tb * TW:(tb + 1) * TW],
                            start=True, stop=True),
                            reads=[rcw, rcst], writes=[rpb[gb]])
                        b.op("act", lambda e, tb=tb, gb=gb: e.activation(
                            bcw16.ap()[:, tb * TW:(tb + 1) * TW], pb(gb, TW), AF.Identity),
                            reads=[rpb[gb]], writes=[rbcw])
                    for js in range(NJS):
                        b1_, b2_ = js % 2, (js + 1) % 2
                        b.op("pool", lambda e, ex=ex, js=js, b1_=b1_: e.dma_start(
                            out=wbig.ap()[:, b1_, :].rearrange("p (a b) -> p a b", a=DC),
                            in_=w1_r[ex, :, :, js * JS:(js + 1) * JS]),
                            writes=[rwb[b1_]], dma=True)
                        b.op("pool", lambda e, ex=ex, js=js, b2_=b2_: e.dma_start(
                            out=wbig.ap()[:, b2_, :].rearrange("p (a b) -> p a b", a=JC),
                            in_=w2_r[ex, js]),
                            writes=[rwb[b2_]], dma=True)
                        for jc in range(JC):
                            jcg = js * JC + jc
                            ht, rht = (pa_t, rpa) if jc % 2 == 0 else (pb_t, rpb)
                            for dc in range(DC):
                                for tb in range(TB):
                                    b.op("pe", lambda e, b1_=b1_, dc=dc, jc=jc, tb=tb, ht=ht: e.matmul(
                                        ht.ap()[:, tb * TW:(tb + 1) * TW],
                                        wbig.ap()[:, b1_, dc * JS + jc * 128:dc * JS + (jc + 1) * 128],
                                        xn16f.ap()[:, dc, tb * TW:(tb + 1) * TW],
                                        start=(dc == 0), stop=(dc == DC - 1)),
                                        reads=[rwb[b1_], rxnf], writes=list(rht),
                                        inc=(dc == DC - 1 and tb == TB - 1))
                            b.op("act", lambda e, ex=ex, jc=jc, jcg=jcg, ht=ht: e.activation(
                                h16.ap()[:, jc, :], ht.ap()[:, 0:S], AF.Gelu,
                                bias=fcst["b1"].ap()[:, ex, jcg:jcg + 1]),
                                reads=list(rht) + [rcst], writes=[rh])
                        for dc in range(DC):
                            yt, ryt = (pa_t, rpa) if dc % 2 == 0 else (pb_t, rpb)
                            for jc in range(JC):
                                for tb in range(TB):
                                    b.op("pe", lambda e, b2_=b2_, dc=dc, jc=jc, tb=tb, yt=yt: e.matmul(
                                        yt.ap()[:, tb * TW:(tb + 1) * TW],
                                        wbig.ap()[:, b2_, jc * 1024 + dc * 128:jc * 1024 + (dc + 1) * 128],
                                        h16.ap()[:, jc, tb * TW:(tb + 1) * TW],
                                        start=(jc == 0), stop=(jc == JC - 1)),
                                        reads=[rwb[b2_], rh], writes=list(ryt),
                                        inc=(jc == JC - 1 and tb == TB - 1))
                            for pz in range(S // 512 if S >= 512 else 1):
                                zw = min(S, 512)
                                zsl = slice(pz * zw, (pz + 1) * zw)
                                b.op("dve", lambda e, yt=yt, zsl=zsl, pz=pz, zw=zw, ryt=ryt: e.tensor_mul(
                                    tmp32.ap()[:, pz % 2, 0:zw], yt.ap()[:, zsl], bcw16.ap()[:, zsl]),
                                    reads=list(ryt) + [rbcw], writes=[rtmp[pz % 2]])
                                b.op("pool", lambda e, dc=dc, zsl=zsl, pz=pz, zw=zw: e.tensor_add(
                                    x32.ap()[:, dc, zsl], x32.ap()[:, dc, zsl], tmp32.ap()[:, pz % 2, 0:zw]),
                                    reads=[rtmp[pz % 2], rx], writes=[rx])

        if 2 in enabled:
            emit_ffn("f1_")
        def emit_attn_b():
            CB = min(S, 256)
            NSL = S // CB
            with contextlib.ExitStack() as ph:
                psb = lambda n_, sh, dt=F32: ph.enter_context(nc.sbuf_tensor("mb_" + n_, sh, dt))
                b.seed(rcst)
                acst = {}
                for name, cdt in [("convw", F32), ("convb", F32), ("dtb", F32), ("dtw", F32),
                                  ("dcol", F32), ("alog", F32), ("selrow16", BF16), ("wxpT", BF16)]:
                    shp = SBUF_SHAPE.get(name, list(WEIGHT_SHAPES[name]))
                    if name == "wxpT":
                        shp = [128, DIC, 33]
                    acst[name] = psb("c_" + name, shp, cdt)
                    src = dw[name].ap()
                    if name == "wxpT":
                        src = src.rearrange("(c p) o -> p c o", p=128)
                    eng_ = "pool" if cdt == BF16 else "sp"
                    b.op(eng_, lambda e, s_=src, n_=name: e.dma_start(out=acst[n_].ap(), in_=s_),
                         writes=[rcst], dma=True)
                A32 = psb("A32", [128, DIC, NST])
                wstr = psb("wstr", [128, 2, 4096], BF16)
                xnb = psb("xnb", [128, 2, DC, CB], BF16)
                xssm = psb("xssm", [128, DIC, CB + 3], BF16)
                zs = psb("zs", [128, DIC, CB], BF16)
                xc16 = psb("xc16", [128, DIC, CB], BF16)
                dt16 = psb("dt16", [128, DIC, CB], BF16)
                scr = psb("scr", [128, 16 * CB], BF16)
                dag = psb("dag", [128, 8, CB])
                brep = psb("brep", [128, NST, CB], BF16)
                crep = psb("crep", [128, NST, CB], BF16)
                h16 = psb("hh16", [128, NST, CB], BF16)
                g16 = psb("gg16", [128, DIC, CB], BF16)
                acc32 = psb("acc32", [128, 2, CB])
                bc16 = psb("bc16", [16, 2, CB], BF16)
                dtr32 = psb("dtr32", [1, CB])
                carry = psb("carry", [128, DIC, NST])
                yr32 = psb("yr32", [128, 2, CB])

                names = ["A32", "wstr0", "wstr1", "xnb0", "xnb1", "xssm", "zs", "xc16", "dt16",
                         "scr", "dag", "brep", "crep", "h16", "g16", "acc0", "acc1", "bc16",
                         "dtr32", "carry", "yr0", "yr1"]
                Rm = {n_: res("mb_" + n_) for n_ in names}
                b.seed(*Rm.values())

                emit_ln_stats()

                b.op("act", lambda e: e.activation(A32.ap(), acst["alog"].ap(), AF.Exp),
                     reads=[rcst], writes=[Rm["A32"]])
                b.op("dve", lambda e: e.tensor_scalar_mul(A32.ap(), A32.ap(), -1.0),
                     reads=[Rm["A32"]], writes=[Rm["A32"]])
                b.op("dve", lambda e: e.memset(carry.ap().rearrange("p a b -> p (a b)"), 0.0),
                     writes=[Rm["carry"]])
                b.op("dve", lambda e: e.memset(xssm.ap()[:, :, 0:3], 0.0), writes=[Rm["xssm"]])

                win_r = dw["winT"].ap().rearrange("(c p) o -> p c o", p=128)
                wout_r = dw["woutT"].ap().rearrange("(c p) o -> p c o", p=128)
                scr_b16 = scr.ap().rearrange("p (n t) -> p n t", n=16)

                def emit_inproj(sl_):
                    T0 = sl_ * CB
                    sp_ = sl_ % 2
                    for dc in range(DC):
                        emit_ln_apply(dc, (T0, T0 + CB), xnb.ap()[:, sp_, dc, :], dc % 2,
                                      Rm["xnb%d" % sp_])
                    if sl_ > 0:
                        b.op("dve", lambda e: e.tensor_copy(xssm.ap()[:, :, 0:3],
                                                            xssm.ap()[:, :, CB:CB + 3]),
                             reads=[Rm["xssm"]], writes=[Rm["xssm"]])
                    for og in range(8):
                        buf = og % 2
                        b.op("pool", lambda e, og=og, buf=buf: e.dma_start(
                            out=wstr.ap()[:, buf, :].rearrange("p (a b) -> p a b", a=DC),
                            in_=win_r[:, :, og * 512:(og + 1) * 512]),
                            writes=[Rm["wstr%d" % buf]], dma=True)
                        for j in range(4):
                            oc = og * 4 + j
                            pslot = oc % 2
                            for dc in range(DC):
                                b.op("pe", lambda e, buf=buf, j=j, dc=dc, pslot=pslot, sp_=sp_: e.matmul(
                                    pa(pslot, CB), wstr.ap()[:, buf, :].rearrange(
                                        "p (a b) -> p a b", a=DC)[:, dc, j * 128:(j + 1) * 128],
                                    xnb.ap()[:, sp_, dc, :], start=(dc == 0), stop=(dc == DC - 1)),
                                    reads=[Rm["wstr%d" % buf], Rm["xnb%d" % sp_], rcst],
                                    writes=[rpa[pslot]], inc=(dc == DC - 1))
                            if oc < 16:
                                b.op("act", lambda e, oc=oc, pslot=pslot: e.activation(
                                    xssm.ap()[:, oc, 3:3 + CB], pa(pslot, CB), AF.Identity),
                                    reads=[rpa[pslot]], writes=[Rm["xssm"]])
                            else:
                                b.op("act", lambda e, oc=oc, pslot=pslot: e.activation(
                                    zs.ap()[:, oc - 16, :], pa(pslot, CB), AF.Identity),
                                    reads=[rpa[pslot]], writes=[Rm["zs"]])

                def emit_ssm(sl_):
                    T0 = sl_ * CB
                    b.op("act", lambda e: e.activation(scr.ap(), zs.ap().rearrange("p a b -> p (a b)"),
                                                       AF.Sigmoid),
                         reads=[Rm["zs"]], writes=[Rm["scr"]])
                    b.op("pool", lambda e: e.tensor_mul(zs.ap().rearrange("p a b -> p (a b)"),
                                                        zs.ap().rearrange("p a b -> p (a b)"), scr.ap()),
                         reads=[Rm["zs"], Rm["scr"]], writes=[Rm["zs"]])
                    for dc in range(DIC):
                        ap_ = dc % 2
                        b.op("dve", lambda e, dc=dc, ap_=ap_: e.tensor_scalar(
                            acc32.ap()[:, ap_, :], xssm.ap()[:, dc, 3:3 + CB],
                            acst["convw"].ap()[:, dc, 3:4], None, ALU.mult),
                            reads=[Rm["xssm"], rcst], writes=[Rm["acc%d" % ap_]])
                        for kk in range(3):
                            b.op("dve", lambda e, dc=dc, kk=kk, ap_=ap_: e.scalar_tensor_tensor(
                                acc32.ap()[:, ap_, :], xssm.ap()[:, dc, kk:kk + CB],
                                acst["convw"].ap()[:, dc, kk:kk + 1], acc32.ap()[:, ap_, :],
                                ALU.mult, ALU.add),
                                reads=[Rm["xssm"], rcst, Rm["acc%d" % ap_]], writes=[Rm["acc%d" % ap_]])
                        b.op("dve", lambda e, dc=dc, ap_=ap_: e.tensor_scalar(
                            acc32.ap()[:, ap_, :], acc32.ap()[:, ap_, :],
                            acst["convb"].ap()[:, dc:dc + 1], None, ALU.add),
                            reads=[rcst, Rm["acc%d" % ap_]], writes=[Rm["acc%d" % ap_]])
                        b.op("act", lambda e, ap_=ap_: e.activation(
                            yr32.ap()[:, ap_, :], acc32.ap()[:, ap_, :], AF.Sigmoid),
                            reads=[Rm["acc%d" % ap_]], writes=[Rm["yr%d" % ap_]])
                        b.op("pool", lambda e, dc=dc, ap_=ap_: e.tensor_mul(
                            xc16.ap()[:, dc, :], acc32.ap()[:, ap_, :], yr32.ap()[:, ap_, :]),
                            reads=[Rm["acc%d" % ap_], Rm["yr%d" % ap_]], writes=[Rm["xc16"]])
                    for kc in range(DIC):
                        last = kc == DIC - 1
                        b.op("pe", lambda e, kc=kc: e.matmul(
                            pb(0, CB, (0, 1)), acst["wxpT"].ap()[:, kc, 0:1], xc16.ap()[:, kc, :],
                            start=(kc == 0), stop=(kc == DIC - 1)),
                            reads=[Rm["xc16"], rcst], writes=[rpb[0]], inc=last)
                        b.op("pe", lambda e, kc=kc: e.matmul(
                            pb(1, CB, (0, 16)), acst["wxpT"].ap()[:, kc, 1:17], xc16.ap()[:, kc, :],
                            start=(kc == 0), stop=(kc == DIC - 1)),
                            reads=[Rm["xc16"], rcst], writes=[rpb[1]], inc=last)
                        b.op("pe", lambda e, kc=kc: e.matmul(
                            pb(2, CB, (0, 16)), acst["wxpT"].ap()[:, kc, 17:33], xc16.ap()[:, kc, :],
                            start=(kc == 0), stop=(kc == DIC - 1)),
                            reads=[Rm["xc16"], rcst], writes=[rpb[2]], inc=last)
                    b.op("act", lambda e: e.activation(dtr32.ap(), pb(0, CB, (0, 1)), AF.Identity),
                         reads=[rpb[0]], writes=[Rm["dtr32"]])
                    b.op("act", lambda e: e.activation(bc16.ap()[:, 0, :], pb(1, CB, (0, 16)), AF.Identity),
                         reads=[rpb[1]], writes=[Rm["bc16"]])
                    b.op("act", lambda e: e.activation(bc16.ap()[:, 1, :], pb(2, CB, (0, 16)), AF.Identity),
                         reads=[rpb[2]], writes=[Rm["bc16"]])
                    for n_ in range(NST):
                        b.op("pe", lambda e, n_=n_: e.matmul(
                            pa(2, CB), acst["selrow16"].ap()[:, n_, :], bc16.ap()[:, 0, :],
                            start=True, stop=True),
                            reads=[Rm["bc16"], rcst], writes=[rpa[2]])
                        b.op("act", lambda e, n_=n_: e.activation(brep.ap()[:, n_, :], pa(2, CB), AF.Identity),
                             reads=[rpa[2]], writes=[Rm["brep"]])
                        b.op("pe", lambda e, n_=n_: e.matmul(
                            pa(2, CB), acst["selrow16"].ap()[:, n_, :], bc16.ap()[:, 1, :],
                            start=True, stop=True),
                            reads=[Rm["bc16"], rcst], writes=[rpa[2]])
                        b.op("act", lambda e, n_=n_: e.activation(crep.ap()[:, n_, :], pa(2, CB), AF.Identity),
                             reads=[rpa[2]], writes=[Rm["crep"]])
                    b.op("pe", lambda e: e.matmul(pb(3, CB), cst["ones_row32"].ap(), dtr32.ap(),
                                                  start=True, stop=True),
                         reads=[Rm["dtr32"], rcst], writes=[rpb[3]])
                    for dc in range(DIC):
                        b.op("dve", lambda e, dc=dc: e.scalar_tensor_tensor(
                            dt16.ap()[:, dc, :], pb(3, CB), acst["dtw"].ap()[:, dc:dc + 1],
                            acst["dtb"].ap()[:, dc:dc + 1].broadcast_to([128, CB]),
                            ALU.mult, ALU.add),
                            reads=[rpb[3], rcst], writes=[Rm["dt16"]])
                    dtf = dt16.ap().rearrange("p a b -> p (a b)")
                    b.op("act", lambda e: e.activation(scr.ap(), dtf, AF.Abs),
                         reads=[Rm["dt16"]], writes=[Rm["scr"]])
                    b.op("act", lambda e: e.activation(scr.ap(), scr.ap(), AF.Exp, scale=-1.0),
                         reads=[Rm["scr"]], writes=[Rm["scr"]])
                    b.op("act", lambda e: e.activation(scr.ap(), scr.ap(), AF.Ln, bias=1.0),
                         reads=[Rm["scr"]], writes=[Rm["scr"]])
                    b.op("act", lambda e: e.activation(g16.ap().rearrange("p a b -> p (a b)"), dtf, AF.Relu),
                         reads=[Rm["dt16"]], writes=[Rm["g16"]])
                    b.op("dve", lambda e: e.tensor_add(dtf, scr.ap(), g16.ap().rearrange("p a b -> p (a b)")),
                         reads=[Rm["scr"], Rm["g16"]], writes=[Rm["dt16"]])
                    for dc in range(DIC):
                        b.op("dve", lambda e, dc=dc: e.tensor_mul(
                            g16.ap()[:, dc, :], dt16.ap()[:, dc, :], xc16.ap()[:, dc, :]),
                            reads=[Rm["dt16"], Rm["xc16"]], writes=[Rm["g16"]])
                        for ng in range(2):
                            for nn in range(8):
                                n_ = ng * 8 + nn
                                b.op("act", lambda e, dc=dc, nn=nn, n_=n_: e.activation(
                                    dag.ap()[:, nn, :], dt16.ap()[:, dc, :], AF.Exp,
                                    scale=A32.ap()[:, dc, n_:n_ + 1]),
                                    reads=[Rm["dt16"], Rm["A32"]], writes=[Rm["dag"]])
                            b.op("dve", lambda e, dc=dc, ng=ng: e.tensor_mul(
                                scr_b16[:, 0:8, :],
                                g16.ap()[:, dc, :].unsqueeze(1).broadcast_to([128, 8, CB]),
                                brep.ap()[:, ng * 8:ng * 8 + 8, :]),
                                reads=[Rm["g16"], Rm["brep"], Rm["scr"]], writes=[Rm["scr"]])
                            for nn in range(8):
                                n_ = ng * 8 + nn
                                b.op("dve", lambda e, dc=dc, nn=nn, n_=n_: e.tensor_tensor_scan(
                                    h16.ap()[:, n_, :], dag.ap()[:, nn, :], scr_b16[:, nn, :],
                                    carry.ap()[:, dc, n_:n_ + 1], ALU.mult, ALU.add),
                                    reads=[Rm["dag"], Rm["scr"], Rm["carry"]], writes=[Rm["h16"]])
                        b.op("dve", lambda e, dc=dc: e.tensor_copy(
                            carry.ap()[:, dc, :], h16.ap()[:, :, CB - 1]),
                            reads=[Rm["h16"]], writes=[Rm["carry"]])
                        b.op("dve", lambda e: e.tensor_mul(scr_b16[:, :, :], h16.ap(), crep.ap()),
                             reads=[Rm["h16"], Rm["crep"], Rm["scr"]], writes=[Rm["scr"]])
                        b.op("dve", lambda e: e.tensor_reduce(
                            yr32.ap()[:, 0, :], scr_b16.transpose([0, 2, 1]), AX.X, ALU.add),
                            reads=[Rm["scr"]], writes=[Rm["yr0"]])
                        b.op("dve", lambda e, dc=dc: e.scalar_tensor_tensor(
                            yr32.ap()[:, 1, :], xc16.ap()[:, dc, :], acst["dcol"].ap()[:, dc:dc + 1],
                            yr32.ap()[:, 0, :], ALU.mult, ALU.add),
                            reads=[Rm["xc16"], rcst, Rm["yr0"]], writes=[Rm["yr1"]])
                        b.op("pool", lambda e, dc=dc: e.tensor_mul(
                            g16.ap()[:, dc, :], yr32.ap()[:, 1, :], zs.ap()[:, dc, :]),
                            reads=[Rm["yr1"], Rm["zs"], Rm["g16"]], writes=[Rm["g16"]])

                def emit_outproj(sl_):
                    T0 = sl_ * CB
                    for og in range(4):
                        buf = og % 2
                        b.op("pool", lambda e, og=og, buf=buf: e.dma_start(
                            out=wstr.ap()[:, buf, :].rearrange("p (a b) -> p a b", a=DIC),
                            in_=wout_r[:, :, og * 256:(og + 1) * 256]),
                            writes=[Rm["wstr%d" % buf]], dma=True)
                        for j in range(2):
                            oc = og * 2 + j
                            pslot = 2 + oc % 2
                            for kc in range(DIC):
                                b.op("pe", lambda e, buf=buf, j=j, kc=kc, pslot=pslot: e.matmul(
                                    pa(pslot, CB), wstr.ap()[:, buf, :].rearrange(
                                        "p (a b) -> p a b", a=DIC)[:, kc, j * 128:(j + 1) * 128],
                                    g16.ap()[:, kc, :], start=(kc == 0), stop=(kc == DIC - 1)),
                                    reads=[Rm["wstr%d" % buf], Rm["g16"], rcst], writes=[rpa[pslot]],
                                    inc=(kc == DIC - 1))
                            b.op("dve", lambda e, oc=oc, pslot=pslot, T0=T0: e.tensor_add(
                                x32.ap()[:, oc, T0:T0 + CB], x32.ap()[:, oc, T0:T0 + CB], pa(pslot, CB)),
                                reads=[rpa[pslot], rx], writes=[rx])

                emit_inproj(0)
                for sl_ in range(NSL):
                    emit_ssm(sl_)
                    if sl_ + 1 < NSL:
                        emit_inproj(sl_ + 1)
                    emit_outproj(sl_)

        if 3 in enabled:
            emit_attn_b()
        if 4 in enabled:
            emit_ffn("f2_")



        stok = b.op("sp", lambda e: e.dma_start(out=xout.ap().rearrange("(c p) t -> p c t", p=128),
                                                in_=x32.ap()),
                    reads=[rx], dma=True)
        for ln in DMA_LANES["sp"]:
            if b.c[ln]:
                nc.sync.wait_ge(sems[ln], b.c[ln])
        for ln in DMA_LANES["pool"]:
            if b.c[ln]:
                nc.gpsimd.wait_ge(sems[ln], b.c[ln])

    return nc


def kernel(**inputs):
    w = prep_weights(inputs)
    x = np.asarray(inputs["L0_attn_a_x"], dtype=np.float32)
    nc = build_nc(S=2048, upto=4)
    in_maps = []
    for c in range(NCORES):
        m = dict(w)
        m["xin"] = np.ascontiguousarray(x[c].T)
        in_maps.append(m)
    res = run_bass_kernel_spmd(nc, in_maps, core_ids=list(range(NCORES)))
    out = np.stack([np.asarray(r["xout"]).T for r in res.results], axis=0)
    return np.ascontiguousarray(out, dtype=np.float32)


# version 14 (final)
# speedup vs baseline: 1.0270x; 1.0270x over previous
import math
import numpy as np
import ml_dtypes
import concourse.bass as bass
import concourse.mybir as mybir
from concourse.bass_utils import run_bass_kernel_spmd

F32 = mybir.dt.float32
BF16 = mybir.dt.bfloat16
AF = mybir.ActivationFunctionType
ALU = mybir.AluOpType
AX = mybir.AxisListType

D = 1024
DC = D // 128
H = 16
HD = 64
WIN = 7
NE = 8
FH = 4096
DI = 2048
DIC = DI // 128
NST = 16
NCORES = 8

ENG = ["pe", "dve", "act", "pool", "sp"]
DMA_LANES = {"sp": [f"dh{i}" for i in range(4)], "pool": [f"ds{i}" for i in range(6)]}
DMA_SEMS = DMA_LANES["sp"] + DMA_LANES["pool"]


class Res:
    __slots__ = ["name", "writer", "readers"]

    def __init__(self, name):
        self.name = name
        self.writer = None
        self.readers = []


class Bld:
    def __init__(self, nc, sems):
        self.nc = nc
        self.sems = sems
        self.c = {k: 0 for k in sems}
        self.water = {e: {k: 0 for k in sems} for e in ENG}
        self.eng = {"pe": nc.tensor, "dve": nc.vector, "act": nc.scalar,
                    "pool": nc.gpsimd, "sp": nc.sync}
        self.pending = {}
        self.lane_rr = {"sp": 0, "pool": 0}
        self.opn = {e: 0 for e in ENG}
        self.strict = False

    SELF_GAP = 8

    def _wait(self, ename, tok):
        if tok is None:
            return
        sem, cnt = tok[0], tok[1]
        if self.water[ename][sem] >= cnt:
            return
        self.eng[ename].wait_ge(self.sems[sem], cnt)
        self.water[ename][sem] = cnt

    def op(self, ename, fn, reads=(), writes=(), dma=False, inc=True):
        e = self.eng[ename]
        if dma:
            lanes = DMA_LANES[ename]
            my_sem = lanes[self.lane_rr[ename] % len(lanes)]
            self.lane_rr[ename] += 1
            self._wait(ename, (my_sem, self.c[my_sem]))
        else:
            my_sem = ename
        myop = self.opn[ename]
        if self.strict:
            for k, v in self.c.items():
                if v > 0:
                    self._wait(ename, (k, v))

        def need_same_edge(tok):
            if ename == "pe" and tok[0] == "pe":
                return False
            if len(tok) < 3:
                return True
            return myop - tok[2] < self.SELF_GAP

        for r in reads:
            if r.writer is not None and (dma or r.writer[0] != ename or need_same_edge(r.writer)):
                self._wait(ename, r.writer)
        for w in writes:
            if w.writer is not None and (dma or w.writer[0] != ename):
                self._wait(ename, w.writer)
            for rd in w.readers:
                if dma or rd[0] != ename:
                    self._wait(ename, rd)
        inst = fn(e)
        self.opn[ename] += 1
        pend = self.pending.setdefault(ename, ([], []))
        if not inc:
            pend[0].extend(reads)
            pend[1].extend(writes)
            return None
        n = 16 if dma else 1
        inst.then_inc(self.sems[my_sem], n)
        self.c[my_sem] += n
        tok = (my_sem, self.c[my_sem], myop)
        for r in list(reads) + pend[0]:
            r.readers.append(tok)
        for w in list(writes) + pend[1]:
            w.writer = tok
            w.readers = []
        self.pending[ename] = ([], [])
        return tok

    def barrier_wait(self, ename, toks):
        for t in toks:
            self._wait(ename, t)

    def fence(self):
        return [(k, v) for k, v in self.c.items() if v > 0]

    def seed(self, *rs):
        f = self.fence()
        for r in rs:
            r.readers.extend(f)


def _f32(x):
    return np.ascontiguousarray(np.asarray(x), dtype=np.float32)


def _bf16np(x):
    return np.ascontiguousarray(np.asarray(x, dtype=np.float32).astype(ml_dtypes.bfloat16))


def prep_weights(i):
    w = {}
    w["wqkvT"] = _f32(np.asarray(i["L0_attn_a_W_qkv"]).T)
    w["bqkv"] = _f32(np.asarray(i["L0_attn_a_b_qkv"]).reshape(24, 128).T)
    w["wlatT"] = _f32(np.tile(np.asarray(i["L0_attn_a_W_lat"]).T, (2, 1)))
    w["blat"] = _f32(np.asarray(i["L0_attn_a_b_lat"]).reshape(64, 1))
    w["wprojT"] = _f32(np.asarray(i["L0_attn_a_W_proj"]).T)
    w["bproj"] = _f32(np.asarray(i["L0_attn_a_b_proj"]).reshape(8, 128).T)
    w["wgateT"] = _f32(np.asarray(i["L0_ffn_w_gate"]).T)
    w["W1"] = _f32(i["L0_ffn_W1"])
    w["W2J"] = _f32(np.asarray(i["L0_ffn_W2"]).reshape(8, 4, 8, 128, 1024)
                    .transpose(0, 1, 3, 2, 4))
    w["b1"] = _f32(np.asarray(i["L0_ffn_b1"]).reshape(8, 32, 128).transpose(2, 0, 1))
    w["b2"] = _f32(i["L0_ffn_b2"])
    w["winT"] = _f32(np.asarray(i["L0_attn_b_in_proj_w"]).T)
    w["convw"] = _f32(np.asarray(i["L0_attn_b_conv_w"]).reshape(16, 128, 4).transpose(1, 0, 2))
    w["convb"] = _f32(np.asarray(i["L0_attn_b_conv_b"]).reshape(16, 128).T)
    w["wxpT"] = _f32(np.asarray(i["L0_attn_b_x_proj_w"]).T)
    w["dtw"] = _f32(np.asarray(i["L0_attn_b_dt_proj_w"]).reshape(16, 128).T)
    w["dtb"] = _f32(np.asarray(i["L0_attn_b_dt_proj_b"]).reshape(16, 128).T)
    w["alog"] = _f32(np.asarray(i["L0_attn_b_A_log"]).reshape(16, 128, 16).transpose(1, 0, 2))
    w["dcol"] = _f32(np.asarray(i["L0_attn_b_D"]).reshape(16, 128).T)
    w["woutT"] = _f32(np.asarray(i["L0_attn_b_out_proj_w"]).T)
    w["ones_col16"] = _bf16np(np.ones((128, 1)))
    w["ones_col32"] = _f32(np.ones((128, 1)))
    w["ones_row32"] = _f32(np.ones((1, 128)))
    w["ones_row16"] = _bf16np(np.ones((1, 128)))
    hm = np.zeros((128, 8, 16), np.float32)
    for c in range(8):
        for p in range(128):
            hm[p, c, 2 * c + p // 64] = 1.0
    w["headmask16"] = _bf16np(hm)
    sv = np.ascontiguousarray(np.transpose(hm, (2, 1, 0)))
    w["selv16"] = _bf16np(sv)
    w["selv32"] = _f32(sv)
    sr = np.zeros((16, 16, 128), np.float32)
    for n in range(16):
        sr[n, n, :] = 1.0
    w["selrow32"] = _f32(sr)
    w["selrow16"] = _bf16np(sr)
    se = np.zeros((8, 8, 128), np.float32)
    for e_ in range(8):
        se[e_, e_, :] = 1.0
    w["sel8"] = _f32(se)
    w["eye32"] = _f32(np.eye(128))
    w["miscc"] = _f32(np.array([[1e-5]]))
    return w


WEIGHT_SHAPES = {
    "wqkvT": (1024, 3072), "bqkv": (128, 24), "wlatT": (128, 64), "blat": (64, 1),
    "wprojT": (1024, 1024), "bproj": (128, 8),
    "wgateT": (1024, 8), "W1": (8, 1024, 4096), "W2J": (8, 4, 128, 8, 1024),
    "b1": (128, 8, 32), "b2": (8, 1024),
    "winT": (1024, 4096), "convw": (128, 16, 4), "convb": (128, 16),
    "wxpT": (2048, 33), "dtw": (128, 16), "dtb": (128, 16),
    "alog": (128, 16, 16), "dcol": (128, 16), "woutT": (2048, 1024),
    "ones_col16": (128, 1), "ones_col32": (128, 1), "ones_row32": (1, 128), "ones_row16": (1, 128),
    "headmask16": (128, 8, 16), "selv16": (16, 8, 128), "selv32": (16, 8, 128),
    "selrow32": (16, 16, 128), "selrow16": (16, 16, 128), "eye32": (128, 128), "miscc": (1, 1), "sel8": (8, 8, 128),
}
BF16_WEIGHTS = {"ones_col16", "headmask16", "selv16", "ones_row16", "selrow16"}
RESIDENT = ["ones_row32", "ones_row16", "eye32", "miscc", "ones_col16", "ones_col32"]


def build_nc(S=2048, upto=4, dbg=False, strict=False, only=None):
    nc = bass.Bass("TRN2", target_bir_lowering=False, detect_race_conditions=False)
    TW = min(S, 512)
    TB = S // TW
    assert S % TW == 0 and S >= 64

    xin = nc.dram_tensor("xin", [D, S], F32, kind="ExternalInput")
    xout = nc.dram_tensor("xout", [D, S], F32, kind="ExternalOutput")
    dw = {name: nc.dram_tensor(name, list(shape), BF16 if name in BF16_WEIGHTS else F32,
                               kind="ExternalInput")
          for name, shape in WEIGHT_SHAPES.items()}
    dbg_out = {}
    if dbg:
        for n_, sh in [("d_eL", [16, 8]), ("d_sL", [16, 8]), ("d_kve", [128, 2 * 8 * 2]),
                       ("d_qe", [128, 8 * 8]), ("d_xne", [128, 8 * 8]), ("d_pe16", [128, 8]),
                       ("d_sL2", [16, 8]), ("d_den2", [16, 256]),
                       ("d_s32a", [16, 7 * 256]), ("d_e16a", [16, 7 * 256]),
                       ("d_s32z", [16, 7 * 256]), ("d_e16z", [16, 7 * 256]),
                       ("d_aoa", [128, 8 * 256]), ("d_aoz", [128, 8 * 256]),
                       ("d_dena", [16, 256]), ("d_denz", [16, 256])]:
            dbg_out[n_] = nc.dram_tensor(n_, sh, F32, kind="ExternalOutput")

    import contextlib
    with contextlib.ExitStack() as st:
        sbt = lambda n_, sh, dt=F32: st.enter_context(nc.sbuf_tensor(n_, sh, dt))
        sems = {k: st.enter_context(nc.semaphore(k)) for k in ENG + DMA_SEMS}
        b = Bld(nc, sems)
        b.strict = strict

        x32 = sbt("x32", [128, DC, S])
        mrow = sbt("mrow", [1, S], BF16)
        rrow = sbt("rrow", [1, S], BF16)
        SBUF_SHAPE = {"wgateT": [128, 8, 8], "wxpT": [128, 16, 33]}
        cst = {name: sbt("c_" + name, SBUF_SHAPE.get(name, list(WEIGHT_SHAPES[name])),
                         BF16 if name in BF16_WEIGHTS else F32) for name in RESIDENT}

        PSW = 2048
        pa_t = st.enter_context(nc.psum_tensor("pa", [128, PSW], F32))
        pb_t = st.enter_context(nc.psum_tensor("pb", [128, PSW], F32))
        BNK = 512

        R = {}

        def res(name):
            if name not in R:
                R[name] = Res(name)
            return R[name]

        rx = res("x32")
        rstat = res("stats")
        rpa = [res(f"pa{i}") for i in range(4)]
        rpb = [res(f"pb{i}") for i in range(4)]
        rcst = res("cst")

        def pa(bank, w=None, rows=(0, 128)):
            w = BNK if w is None else w
            return pa_t.ap()[rows[0]:rows[1], bank * BNK:bank * BNK + w]

        def pb(bank, w=None, rows=(0, 128)):
            w = BNK if w is None else w
            return pb_t.ap()[rows[0]:rows[1], bank * BNK:bank * BNK + w]

        b.op("sp", lambda e: e.dma_start(out=x32.ap(), in_=xin.ap().rearrange("(c p) t -> p c t", p=128)),
             writes=[rx], dma=True)
        for name in RESIDENT:
            src = dw[name].ap()
            if name in SBUF_SHAPE:
                src = src.rearrange("(c p) e -> p c e", p=128)
            b.op("sp", lambda e, s_=src, n_=name: e.dma_start(out=cst[n_].ap(), in_=s_),
                 writes=[rcst], dma=True)
        xb16 = sbt("xb16", [128, 2, TW], BF16)
        sq16 = sbt("sq16", [128, 2, TW], BF16)
        lnt = sbt("lnt", [128, 2, TW])
        rxb = [res("xb0"), res("xb1")]
        rsq = [res("sq0"), res("sq1")]
        rlnt = [res("lnt0"), res("lnt1")]

        def emit_ln_stats(eps=1e-5, mean32=None, mean32_res=None):
            for dc in range(DC):
                for tb in range(TB):
                    pp = (dc * TB + tb) % 2
                    sl = slice(tb * TW, (tb + 1) * TW)
                    b.op("dve", lambda e, pp=pp, dc=dc, sl=sl: e.tensor_copy(
                        xb16.ap()[:, pp, :], x32.ap()[:, dc, sl]),
                        reads=[rx], writes=[rxb[pp]])
                    b.op("act", lambda e, pp=pp: e.activation(sq16.ap()[:, pp, :], xb16.ap()[:, pp, :], AF.Square),
                         reads=[rxb[pp]], writes=[rsq[pp]])
                    last_mm = (dc == DC - 1)
                    b.op("pe", lambda e, sl=sl, pp=pp, dc=dc: e.matmul(
                        pa_t.ap()[0:1, sl], cst["ones_col16"].ap(), xb16.ap()[:, pp, :],
                        start=(dc == 0), stop=last_mm),
                        reads=[rxb[pp], rcst], writes=list(rpa), inc=False)
                    b.op("pe", lambda e, sl=sl, pp=pp, dc=dc: e.matmul(
                        pb_t.ap()[0:1, sl], cst["ones_col16"].ap(), sq16.ap()[:, pp, :],
                        start=(dc == 0), stop=last_mm),
                        reads=[rsq[pp], rcst], writes=list(rpb), inc=True)
            b.op("act", lambda e: e.activation(mrow.ap(), pa_t.ap()[0:1, 0:S], AF.Identity, scale=1.0 / D),
                 reads=list(rpa), writes=[rstat])
            if mean32 is not None:
                b.op("act", lambda e: e.activation(mean32, pa_t.ap()[0:1, 0:S], AF.Identity, scale=1.0 / D),
                     reads=list(rpa), writes=[mean32_res])
            b.op("act", lambda e: e.activation(rrow.ap(), pb_t.ap()[0:1, 0:S], AF.Identity, scale=1.0 / D),
                 reads=list(rpb), writes=[rstat])
            b.op("dve", lambda e: e.tensor_mul(pa_t.ap()[0:1, 0:S], mrow.ap(), mrow.ap()),
                 reads=[rstat], writes=rpa)
            b.op("dve", lambda e: e.tensor_sub(rrow.ap(), rrow.ap(), pa_t.ap()[0:1, 0:S]),
                 reads=rpa, writes=[rstat])
            b.op("act", lambda e: e.activation(rrow.ap(), rrow.ap(), AF.Ln, bias=cst["miscc"].ap()),
                 reads=[rstat, rcst], writes=[rstat])
            b.op("act", lambda e: e.activation(rrow.ap(), rrow.ap(), AF.Exp, scale=-0.5),
                 reads=[rstat], writes=[rstat])

        def emit_bcast_row(row_ap, cols, dst_bankfn, dst_res, w, src_res=None):
            c0, c1 = cols
            lhsT = cst["ones_row16"] if row_ap.dtype == BF16 else cst["ones_row32"]
            for o in range(0, w, 512):
                oe = min(o + 512, w)
                b.op("pe", lambda e, o=o, oe=oe, lhsT=lhsT: e.matmul(
                    dst_bankfn(o, oe), lhsT.ap(), row_ap[0:1, c0 + o:c0 + oe],
                    start=True, stop=True),
                    reads=[rstat if src_res is None else src_res, rcst],
                    writes=[dst_res], inc=(oe == w))

        def emit_ln_apply(dc, cols, out_ap, pp, out_res, extra32=None, extra_res=None):
            c0, c1 = cols
            w = c1 - c0
            assert w <= BNK
            emit_bcast_row(mrow.ap(), cols, lambda o, oe: pa(3, oe)[:, o:oe], rpa[3], w)
            emit_bcast_row(rrow.ap(), cols, lambda o, oe: pb(3, oe)[:, o:oe], rpb[3], w)
            b.op("dve", lambda e: e.tensor_sub(lnt.ap()[:, pp, 0:w], x32.ap()[:, dc, c0:c1], pa(3, w)),
                 reads=[rx, rpa[3]], writes=[rlnt[pp]])
            tok = b.op("dve", lambda e: e.tensor_mul(out_ap, lnt.ap()[:, pp, 0:w], pb(3, w)),
                       reads=[rlnt[pp], rpb[3]], writes=[out_res])
            if extra32 is not None:
                tok = b.op("dve", lambda e: e.tensor_mul(extra32, lnt.ap()[:, pp, 0:w], pb(3, w)),
                           reads=[rlnt[pp], rpb[3]], writes=[extra_res])
            return tok

        enabled = set(only) if only is not None else set(range(1, upto + 1))
        if 1 in enabled:
            with contextlib.ExitStack() as ph:
                psb = lambda n_, sh, dt=F32: ph.enter_context(nc.sbuf_tensor(n_, sh, dt))
                b.seed(rcst)
                acst = {}
                for name in ["bqkv", "blat", "bproj", "headmask16", "selv16", "selv32"]:
                    acst[name] = psb("c_" + name, list(WEIGHT_SHAPES[name]),
                                     BF16 if name in BF16_WEIGHTS else F32)
                    b.op("sp", lambda e, n_=name: e.dma_start(out=acst[n_].ap(), in_=dw[n_].ap()),
                         writes=[rcst], dma=True)
                wlatT16 = psb("wlatT16", [128, 64], BF16)
                b.op("pool", lambda e: e.dma_start(out=wlatT16.ap(), in_=dw["wlatT"].ap()),
                     writes=[rcst], dma=True)

                emit_ln_stats()

                PW = min(S, 256)
                NQ = S // PW
                kv16 = psb("kv16", [128, 2, DC, S], BF16)
                xnS = psb("a_xnS", [128, DC, S], BF16)
                wbuf = psb("a_wbuf", [128, 2, DC, 128], BF16)
                ao16 = psb("a_ao16", [128, 1, DC, PW], BF16)
                acc = psb("a_acc", [128, 2, PW])
                s32 = psb("a_s32", [16, WIN, PW])
                e16 = psb("a_e16", [16, WIN, PW], BF16)
                mx16 = psb("a_mx", [16, PW], BF16)
                den = psb("a_den", [16, PW])
                prod16 = psb("a_prod", [128, 2, PW], BF16)
                q16 = psb("a_q16", [128, PW], BF16)
                eL = psb("a_eL", [16, 8])
                qe16 = psb("a_qe16", [128, DC, 8], BF16)
                xne = psb("a_xne", [128, DC, 8], BF16)
                pe16 = psb("a_pe16", [128, 8], BF16)
                sL = psb("a_sL", [16, 8])
                kve = psb("a_kve", [128, 2, DC, 2], BF16)

                rkv = res("kv16"); rxn = res("a_xnS")
                rwb = [res("a_wb0"), res("a_wb1")]
                rao = [res("a_ao0"), res("a_ao1")]
                racc = [res("a_acc0"), res("a_acc1")]
                rs32 = res("a_s32"); re16 = res("a_e16"); rmx = res("a_mx"); rden = res("a_den")
                rprod = [res("a_prod0"), res("a_prod1")]
                rq = [res("a_q0"), res("a_q0")]; reL = res("a_eL"); rqe = res("a_qe")
                rxne = res("a_xne"); rpe16 = res("a_pe16"); rsL = res("a_sL"); rkve = res("a_kve")

                wqkv_r = dw["wqkvT"].ap().rearrange("(c p) o -> p c o", p=128)
                wproj_r = dw["wprojT"].ap().rearrange("(c p) o -> p c o", p=128)

                def load_w2(buf, src_r, oc0):
                    return b.op("pool", lambda e: e.dma_start(
                        out=wbuf.ap()[:, buf, :, :],
                        in_=src_r[:, :, oc0 * 128:(oc0 + 1) * 128]),
                        writes=[rwb[buf]], dma=True)

                def qkv_group(oc, buf, j, rhs_ap, rhs_res, n, dst_ap, dst_res):
                    for dc in range(DC):
                        b.op("pe", lambda e, dc=dc: e.matmul(
                            dst_ap, wbuf.ap()[:, buf, dc, :],
                            rhs_ap[:, dc, 0:n], start=(dc == 0), stop=(dc == DC - 1)),
                            reads=[rwb[buf], rhs_res, rcst], writes=[dst_res], inc=(dc == DC - 1))

                b.op("dve", lambda e: e.memset(pe16.ap(), 0.0), writes=[rpe16])
                b.op("dve", lambda e: e.memset(xne.ap(), 0.0), writes=[rxne])
                for dc in range(DC):
                    b.op("pe", lambda e: e.matmul(pa(3, 3), cst["ones_row16"].ap(), mrow.ap()[0:1, 0:3],
                                                  start=True, stop=True), reads=[rstat, rcst], writes=[rpa[3]], inc=False)
                    b.op("pe", lambda e: e.matmul(pa(3, 8)[:, 4:7], cst["ones_row16"].ap(), mrow.ap()[0:1, S - 3:S],
                                                  start=True, stop=True), reads=[rstat, rcst], writes=[rpa[3]])
                    b.op("pe", lambda e: e.matmul(pb(3, 3), cst["ones_row16"].ap(), rrow.ap()[0:1, 0:3],
                                                  start=True, stop=True), reads=[rstat, rcst], writes=[rpb[3]], inc=False)
                    b.op("pe", lambda e: e.matmul(pb(3, 8)[:, 4:7], cst["ones_row16"].ap(), rrow.ap()[0:1, S - 3:S],
                                                  start=True, stop=True), reads=[rstat, rcst], writes=[rpb[3]])
                    b.op("dve", lambda e, dc=dc: e.tensor_sub(lnt.ap()[:, 0, 0:3], x32.ap()[:, dc, 0:3], pa(3, 3)),
                         reads=[rx, rpa[3]], writes=[rlnt[0]])
                    b.op("dve", lambda e, dc=dc: e.tensor_sub(lnt.ap()[:, 0, 4:7], x32.ap()[:, dc, S - 3:S],
                                                              pa(3, 8)[:, 4:7]),
                         reads=[rx, rpa[3]], writes=[rlnt[0]])
                    b.op("dve", lambda e, dc=dc: e.tensor_mul(xne.ap()[:, dc, 0:3], lnt.ap()[:, 0, 0:3], pb(3, 3)),
                         reads=[rlnt[0], rpb[3]], writes=[rxne])
                    b.op("dve", lambda e, dc=dc: e.tensor_mul(xne.ap()[:, dc, 4:7], lnt.ap()[:, 0, 4:7],
                                                              pb(3, 8)[:, 4:7]),
                         reads=[rlnt[0], rpb[3]], writes=[rxne])

                for g in range(16):
                    buf = g % 2
                    load_w2(buf, wqkv_r, 8 + g)
                    for j in range(1):
                        oc = 8 + g
                        kvi = (oc - 8) // 8
                        dcv = (oc - 8) % 8
                        for dc in range(DC):
                            b.op("pe", lambda e, dc=dc, buf=buf, j=j: e.matmul(
                                pa(2, 8), wbuf.ap()[:, buf, dc, j * 128:(j + 1) * 128], xne.ap()[:, dc, :],
                                start=(dc == 0), stop=(dc == DC - 1)),
                                reads=[rwb[buf], rxne, rcst], writes=[rpa[2]], inc=(dc == DC - 1))
                        b.op("act", lambda e, kvi=kvi, dcv=dcv, oc=oc: e.activation(
                            kve.ap()[:, kvi, dcv, 0:1], pa(2, 8)[:, 0:1], AF.Identity,
                            bias=acst["bqkv"].ap()[:, oc:oc + 1]),
                            reads=[rpa[2], rcst], writes=[rkve])
                        b.op("act", lambda e, kvi=kvi, dcv=dcv, oc=oc: e.activation(
                            kve.ap()[:, kvi, dcv, 1:2], pa(2, 8)[:, 6:7], AF.Identity,
                            bias=acst["bqkv"].ap()[:, oc:oc + 1]),
                            reads=[rpa[2], rcst], writes=[rkve])
                for kvi in range(2):
                    for dcv in range(DC):
                        for hh in range(2):
                            rows = slice(hh * 64, hh * 64 + 64)
                            b.op("pe", lambda e, kvi=kvi, dcv=dcv, rows=rows: e.matmul(
                                pa(2, 2, (rows.start, rows.start + 64)), wlatT16.ap()[rows, :],
                                kve.ap()[:, kvi, dcv, :][rows, :], start=True, stop=True),
                                reads=[rkve, rcst], writes=[rpa[2]])
                            b.op("act", lambda e, kvi=kvi, dcv=dcv, rows=rows: e.activation(
                                kve.ap()[:, kvi, dcv, :][rows, :],
                                pa(2, 2, (rows.start, rows.start + 64)), AF.Identity,
                                bias=acst["blat"].ap()),
                                reads=[rpa[2], rcst], writes=[rkve])

                for g in range(8):
                    buf = g % 2
                    load_w2(buf, wqkv_r, g)
                    for j in range(1):
                        c = g
                        qkv_group(c, buf, j, xne.ap(), rxne, 8, pa(2, 8), rpa[2])
                        b.op("act", lambda e, c=c: e.activation(qe16.ap()[:, c, :], pa(2, 8), AF.Identity,
                                                                bias=acst["bqkv"].ap()[:, c:c + 1]),
                             reads=[rpa[2], rcst], writes=[rqe])
                for c in range(DC):
                    b.op("dve", lambda e, c=c: e.tensor_mul(pe16.ap()[:, 0:3], qe16.ap()[:, c, 0:3],
                                                            kve.ap()[:, 0, c, 1:2].broadcast_to([128, 3])),
                         reads=[rqe, rkve], writes=[rpe16])
                    b.op("dve", lambda e, c=c: e.tensor_mul(pe16.ap()[:, 4:7], qe16.ap()[:, c, 4:7],
                                                            kve.ap()[:, 0, c, 0:1].broadcast_to([128, 3])),
                         reads=[rqe, rkve], writes=[rpe16])
                    b.op("pe", lambda e, c=c: e.matmul(pb(3, 8, (0, 16)), acst["headmask16"].ap()[:, c, :],
                                                       pe16.ap(), start=(c == 0), stop=(c == DC - 1)),
                        reads=[rpe16, rcst], writes=[rpb[3]], inc=True)
                b.op("act", lambda e: e.activation(eL.ap(), pb(3, 8, (0, 16)), AF.Identity, scale=1.0 / 8.0),
                     reads=[rpb[3]], writes=[reL])
                if dbg:
                    b.op("sp", lambda e: e.dma_start(out=dbg_out["d_eL"].ap(), in_=eL.ap()),
                         reads=[reL], dma=True)
                    b.op("pool", lambda e: e.dma_start(out=dbg_out["d_qe"].ap(),
                         in_=qe16.ap().rearrange("p a b -> p (a b)")), reads=[rqe], dma=True)
                    b.op("pool", lambda e: e.dma_start(out=dbg_out["d_xne"].ap(),
                         in_=xne.ap().rearrange("p a b -> p (a b)")), reads=[rxne], dma=True)
                    b.op("pool", lambda e: e.dma_start(out=dbg_out["d_pe16"].ap(),
                         in_=pe16.ap()), reads=[rpe16], dma=True)
                    b.op("pool", lambda e: e.dma_start(
                        out=dbg_out["d_kve"].ap(),
                        in_=kve.ap().rearrange("p a b c -> p (a b c)")), reads=[rkve], dma=True)

                for tb in range(TB):
                    T0 = tb * TW
                    for dc in range(DC):
                        emit_ln_apply(dc, (T0, T0 + TW), xnS.ap()[:, dc, T0:T0 + TW], dc % 2, rxn)
                for g in range(16):
                    buf = g % 2
                    load_w2(buf, wqkv_r, 8 + g)
                    for j in range(1):
                        oc = 8 + g
                        kvi = (oc - 8) // 8
                        dcv = (oc - 8) % 8
                        for tb in range(TB):
                            T0 = tb * TW
                            sl = slice(T0, T0 + TW)
                            pslot = (g + tb) % 3
                            qkv_group(oc, buf, j, xnS.ap()[:, :, sl], rxn, TW, pb(pslot, TW), rpb[pslot])
                            b.op("act", lambda e, kvi=kvi, dcv=dcv, sl=sl, oc=oc, pslot=pslot: e.activation(
                                kv16.ap()[:, kvi, dcv, sl], pb(pslot, TW), AF.Identity,
                                bias=acst["bqkv"].ap()[:, oc:oc + 1]),
                                reads=[rpb[pslot], rcst], writes=[rkv])
                for kvi in range(2):
                    for dcv in range(DC):
                        for hh in range(2):
                            rows = slice(hh * 64, hh * 64 + 64)
                            for tb in range(TB):
                                sl = slice(tb * TW, (tb + 1) * TW)
                                pslot = tb % 2
                                b.op("pe", lambda e, kvi=kvi, dcv=dcv, rows=rows, sl=sl, pslot=pslot: e.matmul(
                                    pa(pslot, TW, (rows.start, rows.start + 64)), wlatT16.ap()[rows, :],
                                    kv16.ap()[:, kvi, dcv, sl][rows, :], start=True, stop=True),
                                    reads=[rkv, rcst], writes=[rpa[pslot]])
                                b.op("act", lambda e, kvi=kvi, dcv=dcv, rows=rows, sl=sl, pslot=pslot: e.activation(
                                    kv16.ap()[:, kvi, dcv, sl][rows, :],
                                    pa(pslot, TW, (rows.start, rows.start + 64)), AF.Identity,
                                    bias=acst["blat"].ap()),
                                    reads=[rpa[pslot], rcst], writes=[rkv])

                DLT = list(range(-3, 4))
                sc_bank = [(pa_t, rpa, 0), (pa_t, rpa, 1), (pa_t, rpa, 2), (pa_t, rpa, 3),
                           (pb_t, rpb, 0), (pb_t, rpb, 1), (pb_t, rpb, 2)]

                def sc_ap(dd, a0, a1):
                    t_, _, k_ = sc_bank[dd]
                    return t_.ap()[0:16, k_ * BNK + a0:k_ * BNK + a1]

                for tq in range(NQ):
                    T0 = tq * PW
                    sl = slice(T0, T0 + PW)
                    app = 0
                    for g in range(8):
                        buf = g % 2
                        load_w2(buf, wqkv_r, g)
                        for j in range(1):
                            c = g
                            qkv_group(c, buf, j, xnS.ap()[:, :, sl], rxn, PW, pb(3, PW), rpb[3])
                            cq = c % 2
                            b.op("act", lambda e, c=c, cq=cq: e.activation(q16.ap(), pb(3, PW),
                                                                    AF.Identity,
                                                                    bias=acst["bqkv"].ap()[:, c:c + 1]),
                                 reads=[rpb[3], rcst], writes=[rq[cq]])
                            for dd, dlt in enumerate(DLT):
                                lo = max(0, T0 + dlt)
                                hi = min(S, T0 + PW + dlt)
                                if hi <= lo:
                                    continue
                                a0, a1 = lo - dlt - T0, hi - dlt - T0
                                dp = dd % 2
                                b.op("dve", lambda e, a0=a0, a1=a1, lo=lo, hi=hi, c=c, cq=cq, dp=dp: e.tensor_mul(
                                    prod16.ap()[:, dp, a0:a1], q16.ap()[:, a0:a1], kv16.ap()[:, 0, c, lo:hi]),
                                    reads=[rq[cq], rkv], writes=[rprod[dp]])
                                t_, rl_, k_ = sc_bank[dd]
                                b.op("pe", lambda e, t_=t_, k_=k_, a0=a0, a1=a1, c=c, dp=dp: e.matmul(
                                    t_.ap()[0:16, k_ * BNK + a0:k_ * BNK + a1], acst["headmask16"].ap()[:, c, :],
                                    prod16.ap()[:, dp, a0:a1], start=(c == 0), stop=(c == DC - 1)),
                                    reads=[rprod[dp], rcst], writes=[rl_[k_]], inc=True)
                    for dd in range(WIN):
                        dlt = dd - 3
                        v0 = max(0, -(T0 + dlt))
                        v1 = PW - max(0, T0 + PW + dlt - S)
                        t_, rl_, k_ = sc_bank[dd]
                        b.op("act", lambda e, dd=dd, v0=v0, v1=v1: e.activation(
                            s32.ap()[:, dd, v0:v1], sc_ap(dd, v0, v1), AF.Identity, scale=1.0 / 8.0),
                             reads=[rl_[k_]], writes=[rs32])
                    if tq == 0:
                        for dd in range(3):
                            ncols = 3 - dd
                            b.op("dve", lambda e, dd=dd, ncols=ncols: e.tensor_copy(
                                s32.ap()[:, dd, 0:ncols], eL.ap()[:, 0:ncols]),
                                reads=[reL], writes=[rs32])
                    if tq == NQ - 1:
                        for dd in range(4, 7):
                            ncols = dd - 3
                            b.op("dve", lambda e, dd=dd, ncols=ncols: e.tensor_copy(
                                s32.ap()[:, dd, PW - ncols:PW], eL.ap()[:, 7 - ncols:7]),
                                reads=[reL], writes=[rs32])
                    b.op("dve", lambda e: e.tensor_max(mx16.ap(), s32.ap()[:, 0, :], s32.ap()[:, 1, :]),
                         reads=[rs32], writes=[rmx])
                    for dd in range(2, WIN):
                        b.op("dve", lambda e, dd=dd: e.tensor_max(mx16.ap(), mx16.ap(), s32.ap()[:, dd, :]),
                             reads=[rs32, rmx], writes=[rmx])
                    for dd in range(WIN):
                        b.op("dve", lambda e, dd=dd: e.tensor_sub(s32.ap()[:, dd, :], s32.ap()[:, dd, :], mx16.ap()),
                             reads=[rmx, rs32], writes=[rs32])
                    for dd in range(WIN):
                        b.op("act", lambda e, dd=dd: e.activation(e16.ap()[:, dd, :], s32.ap()[:, dd, :], AF.Exp),
                             reads=[rs32], writes=[re16])
                    b.op("dve", lambda e: e.tensor_add(den.ap(), e16.ap()[:, 0, :], e16.ap()[:, 1, :]),
                         reads=[re16], writes=[rden])
                    for dd in range(2, WIN):
                        b.op("dve", lambda e, dd=dd: e.tensor_add(den.ap(), den.ap(), e16.ap()[:, dd, :]),
                             reads=[re16, rden], writes=[rden])
                    b.op("dve", lambda e: e.reciprocal(den.ap(), den.ap()), reads=[rden], writes=[rden])
                    if tq == 0:
                        b.op("dve", lambda e: e.tensor_add(sL.ap()[:, 3:4], e16.ap()[:, 0, 0:1], e16.ap()[:, 1, 0:1]),
                             reads=[re16], writes=[rsL])
                        b.op("dve", lambda e: e.tensor_add(sL.ap()[:, 0:1], sL.ap()[:, 3:4], e16.ap()[:, 2, 0:1]),
                             reads=[re16, rsL], writes=[rsL])
                        b.op("dve", lambda e: e.tensor_add(sL.ap()[:, 1:2], e16.ap()[:, 0, 1:2], e16.ap()[:, 1, 1:2]),
                             reads=[re16], writes=[rsL])
                        b.op("dve", lambda e: e.tensor_copy(sL.ap()[:, 2:3], e16.ap()[:, 0, 2:3]),
                             reads=[re16], writes=[rsL])
                        if dbg:
                            b.op("sp", lambda e: e.dma_start(out=dbg_out["d_sL2"].ap(), in_=sL.ap()),
                                 reads=[rsL], dma=True)
                            b.op("sp", lambda e: e.dma_start(out=dbg_out["d_den2"].ap(), in_=den.ap()),
                                 reads=[rden], dma=True)
                    if tq == NQ - 1:
                        b.op("dve", lambda e: e.tensor_add(sL.ap()[:, 7:8], e16.ap()[:, 6, PW - 1:PW],
                                                           e16.ap()[:, 5, PW - 1:PW]),
                             reads=[re16], writes=[rsL])
                        b.op("dve", lambda e: e.tensor_add(sL.ap()[:, 6:7], sL.ap()[:, 7:8], e16.ap()[:, 4, PW - 1:PW]),
                             reads=[re16, rsL], writes=[rsL])
                        b.op("dve", lambda e: e.tensor_add(sL.ap()[:, 5:6], e16.ap()[:, 6, PW - 2:PW - 1],
                                                           e16.ap()[:, 5, PW - 2:PW - 1]),
                             reads=[re16], writes=[rsL])
                        b.op("dve", lambda e: e.tensor_copy(sL.ap()[:, 4:5], e16.ap()[:, 6, PW - 3:PW - 2]),
                             reads=[re16], writes=[rsL])

                    if dbg and tq in (0, NQ - 1):
                        sfx = "a" if tq == 0 else "z"
                        b.op("sp", lambda e, sfx=sfx: e.dma_start(
                            out=dbg_out["d_s32" + sfx].ap(), in_=s32.ap().rearrange("h w t -> h (w t)")),
                            reads=[rs32], dma=True)
                        b.op("pool", lambda e, sfx=sfx: e.dma_start(
                            out=dbg_out["d_e16" + sfx].ap(), in_=e16.ap().rearrange("h w t -> h (w t)")),
                            reads=[re16], dma=True)
                        b.op("sp", lambda e, sfx=sfx: e.dma_start(
                            out=dbg_out["d_den" + sfx].ap(), in_=den.ap()), reads=[rden], dma=True)
                        if tq == NQ - 1:
                            b.op("sp", lambda e: e.dma_start(out=dbg_out["d_sL"].ap(), in_=sL.ap()),
                                 reads=[rsL], dma=True)
                    for c in range(DC):
                        cp = c % 2
                        first = True
                        for dd, dlt in enumerate(DLT):
                            lo = max(0, T0 + dlt)
                            hi = min(S, T0 + PW + dlt)
                            if hi <= lo:
                                continue
                            a0, a1 = lo - dlt - T0, hi - dlt - T0
                            pslot = dd % 2
                            b.op("pe", lambda e, pslot=pslot, c=c, dd=dd, a0=a0, a1=a1: e.matmul(
                                pa(pslot, a1 - a0), acst["selv16"].ap()[:, c, :], e16.ap()[:, dd, a0:a1],
                                start=True, stop=True),
                                reads=[re16, rcst], writes=[rpa[pslot]])
                            dp = dd % 2
                            b.op("dve", lambda e, pslot=pslot, a0=a0, a1=a1, c=c, lo=lo, hi=hi, dp=dp: e.tensor_mul(
                                prod16.ap()[:, dp, a0:a1], pa(pslot, a1 - a0), kv16.ap()[:, 1, c, lo:hi]),
                                reads=[rpa[pslot], rkv], writes=[rprod[dp]])
                            if first:
                                if a0 > 0:
                                    b.op("pool", lambda e, cp=cp, a0=a0: e.memset(acc.ap()[:, cp, 0:a0], 0.0),
                                         writes=[racc[cp]])
                                if a1 < PW:
                                    b.op("pool", lambda e, cp=cp, a1=a1: e.memset(acc.ap()[:, cp, a1:PW], 0.0),
                                         writes=[racc[cp]])
                                b.op("pool", lambda e, cp=cp, a0=a0, a1=a1, dp=dp: e.tensor_copy(
                                    acc.ap()[:, cp, a0:a1], prod16.ap()[:, dp, a0:a1]),
                                    reads=[rprod[dp]], writes=[racc[cp]])
                                first = False
                            else:
                                b.op("pool", lambda e, cp=cp, a0=a0, a1=a1, dp=dp: e.tensor_add(
                                    acc.ap()[:, cp, a0:a1], acc.ap()[:, cp, a0:a1], prod16.ap()[:, dp, a0:a1]),
                                    reads=[rprod[dp], racc[cp]], writes=[racc[cp]])
                        for cond, ec0, col0, vcol in [(tq == 0, 0, 0, 1), (tq == NQ - 1, 4, PW - 3, 0)]:
                            if not cond:
                                continue
                            b.op("pe", lambda e, c=c, ec0=ec0: e.matmul(
                                pa(2, 3), acst["selv32"].ap()[:, c, :], sL.ap()[:, ec0:ec0 + 3],
                                start=True, stop=True),
                                reads=[rsL, rcst], writes=[rpa[2]])
                            b.op("dve", lambda e, c=c, vcol=vcol: e.tensor_mul(
                                lnt.ap()[:, 0, 0:3], pa(2, 3),
                                kve.ap()[:, 1, c, vcol:vcol + 1].broadcast_to([128, 3])),
                                reads=[rpa[2], rkve], writes=[rlnt[0]])
                            b.op("pool", lambda e, c=c, cp=cp, col0=col0: e.tensor_add(
                                acc.ap()[:, cp, col0:col0 + 3], acc.ap()[:, cp, col0:col0 + 3],
                                lnt.ap()[:, 0, 0:3]),
                                reads=[rlnt[0], racc[cp]], writes=[racc[cp]])
                        b.op("pe", lambda e, c=c: e.matmul(pa(3, PW), acst["selv32"].ap()[:, c, :], den.ap(),
                                                           start=True, stop=True),
                             reads=[rden, rcst], writes=[rpa[3]])
                        b.op("dve", lambda e, c=c, cp=cp, app=app: e.tensor_mul(
                            ao16.ap()[:, app, c, :], acc.ap()[:, cp, :], pa(3, PW)),
                             reads=[racc[cp], rpa[3]], writes=[rao[app]])

                    if dbg and tq in (0, NQ - 1):
                        sfx = "a" if tq == 0 else "z"
                        b.op("pool", lambda e, sfx=sfx, app=app: e.dma_start(
                            out=dbg_out["d_ao" + sfx].ap(),
                            in_=ao16.ap()[:, app].rearrange("p c t -> p (c t)")),
                            reads=[rao[app]], dma=True)
                    for g in range(8):
                        buf = g % 2
                        load_w2(buf, wproj_r, g)
                        for j in range(1):
                            oc = g
                            pslot = oc % 2
                            for c in range(DC):
                                b.op("pe", lambda e, buf=buf, c=c, app=app, pslot=pslot: e.matmul(
                                    pb(pslot, PW), wbuf.ap()[:, buf, c, :],
                                    ao16.ap()[:, app, c, :], start=(c == 0), stop=(c == DC - 1)),
                                    reads=[rwb[buf], rao[app], rcst], writes=[rpb[pslot]], inc=(c == DC - 1))
                            b.op("act", lambda e, oc=oc, pslot=pslot: e.activation(
                                lnt.ap()[:, oc % 2, 0:PW], pb(pslot, PW), AF.Identity,
                                bias=acst["bproj"].ap()[:, oc:oc + 1]),
                                reads=[rpb[pslot], rcst], writes=[rlnt[oc % 2]])
                            b.op("dve", lambda e, oc=oc, sl=sl: e.tensor_add(
                                x32.ap()[:, oc, sl], x32.ap()[:, oc, sl], lnt.ap()[:, oc % 2, 0:PW]),
                                reads=[rlnt[oc % 2], rx], writes=[rx])


        def emit_ffn(tag):
            with contextlib.ExitStack() as ph:
                psb = lambda n_, sh, dt=F32: ph.enter_context(nc.sbuf_tensor(tag + n_, sh, dt))
                b.seed(rcst)
                fcst = {}
                for name, cdt in [("wgateT", F32), ("b1", F32), ("b2", BF16), ("sel8", BF16)]:
                    shp = SBUF_SHAPE.get(name, list(WEIGHT_SHAPES[name]))
                    fcst[name] = psb("c_" + name, shp, cdt)
                    src = dw[name].ap()
                    if name in SBUF_SHAPE:
                        src = src.rearrange("(c p) e -> p c e", p=128)
                    eng_ = "pool" if cdt == BF16 else "sp"
                    b.op(eng_, lambda e, s_=src, n_=name: e.dma_start(out=fcst[n_].ap(), in_=s_),
                         writes=[rcst], dma=True)

                mrow32 = psb("mrow32", [1, S])
                rm32 = res(tag + "m32")
                rxnf = res(tag + "xn16f")
                rcw = res(tag + "cw16")
                b.seed(rxnf, rcw, rm32, res(tag + "g32"), res(tag + "gt"), res(tag + "gr"),
                       res(tag + "ws8"))
                emit_ln_stats(mean32=mrow32.ap(), mean32_res=rm32)

                JS = 1024
                NJS = FH // JS
                JC = JS // 128
                xn16f = psb("xn16f", [128, DC, S], BF16)
                cw16 = psb("cw16", [8, S], BF16)
                rbcw = res(tag + "bcw16")
                rwb = [res(tag + "wb0"), res(tag + "wb1")]
                rh = res(tag + "h16")
                rtmp = [res(tag + "tmp0"), res(tag + "tmp1")]

                for tb in range(TB):
                    T0 = tb * TW
                    for dc in range(DC):
                        emit_ln_apply(dc, (T0, T0 + TW), xn16f.ap()[:, dc, T0:T0 + TW], dc % 2, rxnf)
                with contextlib.ExitStack() as gph:
                    g32 = gph.enter_context(nc.sbuf_tensor(tag + "g32", [8, S], F32))
                    gt = gph.enter_context(nc.sbuf_tensor(tag + "gt", [128, 12, 8], F32))
                    gr = gph.enter_context(nc.sbuf_tensor(tag + "gr", [128, 6], F32))
                    ws8 = gph.enter_context(nc.sbuf_tensor(tag + "ws8", [8, 2], F32))
                    wsr = gph.enter_context(nc.sbuf_tensor(tag + "wsr", [1, 8], F32))
                    rg = res(tag + "g32")
                    rgt = res(tag + "gt")
                    rgr = res(tag + "gr")
                    rws = res(tag + "ws8")
                    for dc in range(DC):
                        b.op("pe", lambda e, dc=dc: e.matmul(
                            pb(3, 1, (0, 8)), fcst["wgateT"].ap()[:, dc, :], cst["ones_col32"].ap(),
                            start=True, stop=True),
                            reads=[rcst], writes=[rpb[3]])
                        if dc == 0:
                            b.op("dve", lambda e: e.tensor_copy(ws8.ap()[:, 0:1], pb(3, 1, (0, 8))),
                                 reads=[rpb[3]], writes=[rws])
                        else:
                            b.op("dve", lambda e: e.tensor_add(ws8.ap()[:, 0:1], ws8.ap()[:, 0:1],
                                                               pb(3, 1, (0, 8))),
                                 reads=[rpb[3], rws], writes=[rws])
                    b.op("pe", lambda e: e.transpose(pb(3, 8, (0, 1)), ws8.ap()[:, 0:1],
                                                     cst["eye32"].ap()[0:8, 0:8]),
                         reads=[rws, rcst], writes=[rpb[3]])
                    b.op("dve", lambda e: e.tensor_copy(wsr.ap(), pb(3, 8, (0, 1))),
                         reads=[rpb[3]], writes=[rws])
                    for tb in range(TB):
                        T0 = tb * TW
                        gb = tb % 2
                        for dc in range(DC):
                            b.op("pe", lambda e, dc=dc, tb=tb, gb=gb: e.matmul(
                                pa(gb, TW, (0, 8)), fcst["wgateT"].ap()[:, dc, :],
                                x32.ap()[:, dc, tb * TW:(tb + 1) * TW], start=True, stop=True),
                                reads=[rx, rcst], writes=[rpa[gb]])
                            if dc == 0:
                                b.op("dve", lambda e, tb=tb, gb=gb: e.tensor_copy(
                                    g32.ap()[:, tb * TW:(tb + 1) * TW], pa(gb, TW, (0, 8))),
                                    reads=[rpa[gb]], writes=[rg])
                            else:
                                b.op("dve", lambda e, tb=tb, gb=gb: e.tensor_add(
                                    g32.ap()[:, tb * TW:(tb + 1) * TW], g32.ap()[:, tb * TW:(tb + 1) * TW],
                                    pa(gb, TW, (0, 8))),
                                    reads=[rpa[gb], rg], writes=[rg])
                        b.op("pe", lambda e, tb=tb, gb=gb: e.matmul(
                            pb(gb, TW, (0, 8)), wsr.ap(), mrow32.ap()[0:1, tb * TW:(tb + 1) * TW],
                            start=True, stop=True),
                            reads=[rws, rm32], writes=[rpb[gb]])
                        b.op("dve", lambda e, tb=tb, gb=gb: e.tensor_sub(
                            g32.ap()[:, tb * TW:(tb + 1) * TW], g32.ap()[:, tb * TW:(tb + 1) * TW],
                            pb(gb, TW, (0, 8))),
                            reads=[rpb[gb], rg], writes=[rg])
                        b.op("pe", lambda e, tb=tb, gb=gb: e.matmul(
                            pb(gb, TW, (0, 8)), cst["ones_row16"].ap()[0:1, 0:8],
                            rrow.ap()[0:1, tb * TW:(tb + 1) * TW], start=True, stop=True),
                            reads=[rstat, rcst], writes=[rpb[gb]])
                        b.op("dve", lambda e, tb=tb, gb=gb: e.tensor_mul(
                            g32.ap()[:, tb * TW:(tb + 1) * TW], g32.ap()[:, tb * TW:(tb + 1) * TW],
                            pb(gb, TW, (0, 8))),
                            reads=[rpb[gb], rg], writes=[rg])

                    NT = S // 128
                    for t_ in range(NT):
                        ti = t_ % 8
                        cols = slice(t_ * 128, (t_ + 1) * 128)
                        b.op("pe", lambda e, cols=cols: e.transpose(
                            pb(3, 8), g32.ap()[0:8, cols], cst["eye32"].ap()[0:8, 0:8]),
                            reads=[rg, rcst], writes=[rpb[3]])
                        b.op("dve", lambda e, ti=ti: e.tensor_copy(gt.ap()[:, ti, :], pb(3, 8)),
                             reads=[rpb[3]], writes=[rgt])
                        b.op("dve", lambda e, ti=ti: e.reduce_max(gr.ap()[:, 0:1], gt.ap()[:, ti, :], AX.X),
                             reads=[rgt], writes=[rgr])
                        b.op("dve", lambda e, ti=ti: e.tensor_scalar(
                            gt.ap()[:, ti + 1 if False else 9, :], gt.ap()[:, ti, :], gr.ap()[:, 0:1], None,
                            ALU.subtract),
                            reads=[rgt, rgr], writes=[rgt])
                        b.op("act", lambda e: e.activation(gt.ap()[:, 10, :], gt.ap()[:, 9, :], AF.Exp),
                             reads=[rgt], writes=[rgt])
                        b.op("dve", lambda e: e.reduce_sum(gr.ap()[:, 1:2], gt.ap()[:, 10, :], AX.X),
                             reads=[rgt], writes=[rgr])
                        b.op("dve", lambda e: e.reciprocal(gr.ap()[:, 2:3], gr.ap()[:, 1:2]),
                             reads=[rgr], writes=[rgr])
                        b.op("dve", lambda e, ti=ti: e.tensor_scalar(
                            gt.ap()[:, 11, :], gt.ap()[:, ti, :], gr.ap()[:, 0:1], None, ALU.is_equal),
                            reads=[rgt, rgr], writes=[rgt])
                        b.op("dve", lambda e, ti=ti: e.scalar_tensor_tensor(
                            gt.ap()[:, 9, :], gt.ap()[:, 11, :], -1e30, gt.ap()[:, ti, :],
                            ALU.mult, ALU.add),
                            reads=[rgt], writes=[rgt])
                        b.op("dve", lambda e: e.reduce_max(gr.ap()[:, 3:4], gt.ap()[:, 9, :], AX.X),
                             reads=[rgt], writes=[rgr])
                        b.op("dve", lambda e: e.tensor_scalar(
                            gt.ap()[:, 9, :], gt.ap()[:, 9, :], gr.ap()[:, 3:4], None, ALU.is_equal),
                            reads=[rgt, rgr], writes=[rgt])
                        b.op("dve", lambda e: e.tensor_add(gt.ap()[:, 9, :], gt.ap()[:, 9, :], gt.ap()[:, 11, :]),
                             reads=[rgt], writes=[rgt])
                        b.op("dve", lambda e: e.tensor_scalar(
                            gt.ap()[:, 10, :], gt.ap()[:, 10, :], gr.ap()[:, 2:3], None, ALU.mult),
                            reads=[rgt, rgr], writes=[rgt])
                        b.op("dve", lambda e: e.tensor_mul(gt.ap()[:, 10, :], gt.ap()[:, 10, :], gt.ap()[:, 9, :]),
                             reads=[rgt], writes=[rgt])
                        b.op("pe", lambda e: e.transpose(
                            pb(3, 128, (0, 8)), gt.ap()[:, 10, :], cst["eye32"].ap()),
                            reads=[rgt, rcst], writes=[rpb[3]])
                        b.op("dve", lambda e, cols=cols: e.tensor_copy(cw16.ap()[:, cols], pb(3, 128, (0, 8))),
                             reads=[rpb[3]], writes=[rcw])

                for dc in range(DC):
                    for tb in range(TB):
                        gb = (dc * TB + tb) % 2
                        b.op("pe", lambda e, dc=dc, tb=tb, gb=gb: e.matmul(
                            pa(gb, TW), fcst["b2"].ap()[:, dc * 128:(dc + 1) * 128],
                            cw16.ap()[:, tb * TW:(tb + 1) * TW], start=True, stop=True),
                            reads=[rcw, rcst], writes=[rpa[gb]])
                        b.op("dve", lambda e, dc=dc, tb=tb, gb=gb: e.tensor_add(
                            x32.ap()[:, dc, tb * TW:(tb + 1) * TW], x32.ap()[:, dc, tb * TW:(tb + 1) * TW],
                            pa(gb, TW)),
                            reads=[rpa[gb], rx], writes=[rx])

                bcw16 = psb("bcw16", [128, S], BF16)
                wbig = psb("wbig", [128, 2, 8192], BF16)
                h16 = psb("h16", [128, JC, S], BF16)
                tmp32 = psb("tmp32", [128, 2, 512])
                b.seed(rbcw, rwb[0], rwb[1], rh, rtmp[0], rtmp[1])
                w1_r = dw["W1"].ap().rearrange("e (c p) j -> e p c j", p=128)
                w2_r = dw["W2J"].ap()
                wb_tok = [None, None]
                for ex in range(NE):
                    for tb in range(TB):
                        gb = tb % 2
                        b.op("pe", lambda e, ex=ex, tb=tb, gb=gb: e.matmul(
                            pb(gb, TW), fcst["sel8"].ap()[:, ex, :], cw16.ap()[:, tb * TW:(tb + 1) * TW],
                            start=True, stop=True),
                            reads=[rcw, rcst], writes=[rpb[gb]])
                        b.op("act", lambda e, tb=tb, gb=gb: e.activation(
                            bcw16.ap()[:, tb * TW:(tb + 1) * TW], pb(gb, TW), AF.Identity),
                            reads=[rpb[gb]], writes=[rbcw])
                    for js in range(NJS):
                        b1_, b2_ = js % 2, (js + 1) % 2
                        b.op("pool", lambda e, ex=ex, js=js, b1_=b1_: e.dma_start(
                            out=wbig.ap()[:, b1_, :].rearrange("p (a b) -> p a b", a=DC),
                            in_=w1_r[ex, :, :, js * JS:(js + 1) * JS]),
                            writes=[rwb[b1_]], dma=True)
                        b.op("pool", lambda e, ex=ex, js=js, b2_=b2_: e.dma_start(
                            out=wbig.ap()[:, b2_, :].rearrange("p (a b) -> p a b", a=JC),
                            in_=w2_r[ex, js]),
                            writes=[rwb[b2_]], dma=True)
                        for jc in range(JC):
                            jcg = js * JC + jc
                            ht, rht = (pa_t, rpa) if jc % 2 == 0 else (pb_t, rpb)
                            for dc in range(DC):
                                for tb in range(TB):
                                    b.op("pe", lambda e, b1_=b1_, dc=dc, jc=jc, tb=tb, ht=ht: e.matmul(
                                        ht.ap()[:, tb * TW:(tb + 1) * TW],
                                        wbig.ap()[:, b1_, dc * JS + jc * 128:dc * JS + (jc + 1) * 128],
                                        xn16f.ap()[:, dc, tb * TW:(tb + 1) * TW],
                                        start=(dc == 0), stop=(dc == DC - 1)),
                                        reads=[rwb[b1_], rxnf], writes=list(rht),
                                        inc=(dc == DC - 1 and tb == TB - 1))
                            b.op("act", lambda e, ex=ex, jc=jc, jcg=jcg, ht=ht: e.activation(
                                h16.ap()[:, jc, :], ht.ap()[:, 0:S], AF.Gelu,
                                bias=fcst["b1"].ap()[:, ex, jcg:jcg + 1]),
                                reads=list(rht) + [rcst], writes=[rh])
                        for dc in range(DC):
                            yt, ryt = (pa_t, rpa) if dc % 2 == 0 else (pb_t, rpb)
                            for jc in range(JC):
                                for tb in range(TB):
                                    b.op("pe", lambda e, b2_=b2_, dc=dc, jc=jc, tb=tb, yt=yt: e.matmul(
                                        yt.ap()[:, tb * TW:(tb + 1) * TW],
                                        wbig.ap()[:, b2_, jc * 1024 + dc * 128:jc * 1024 + (dc + 1) * 128],
                                        h16.ap()[:, jc, tb * TW:(tb + 1) * TW],
                                        start=(jc == 0), stop=(jc == JC - 1)),
                                        reads=[rwb[b2_], rh], writes=list(ryt),
                                        inc=(jc == JC - 1 and tb == TB - 1))
                            for pz in range(S // 512 if S >= 512 else 1):
                                zw = min(S, 512)
                                zsl = slice(pz * zw, (pz + 1) * zw)
                                b.op("dve", lambda e, yt=yt, zsl=zsl, pz=pz, zw=zw, ryt=ryt: e.tensor_mul(
                                    tmp32.ap()[:, pz % 2, 0:zw], yt.ap()[:, zsl], bcw16.ap()[:, zsl]),
                                    reads=list(ryt) + [rbcw], writes=[rtmp[pz % 2]])
                                b.op("pool", lambda e, dc=dc, zsl=zsl, pz=pz, zw=zw: e.tensor_add(
                                    x32.ap()[:, dc, zsl], x32.ap()[:, dc, zsl], tmp32.ap()[:, pz % 2, 0:zw]),
                                    reads=[rtmp[pz % 2], rx], writes=[rx])

        if 2 in enabled:
            emit_ffn("f1_")
        def emit_attn_b():
            CB = min(S, 256)
            NSL = S // CB
            with contextlib.ExitStack() as ph:
                psb = lambda n_, sh, dt=F32: ph.enter_context(nc.sbuf_tensor("mb_" + n_, sh, dt))
                b.seed(rcst)
                acst = {}
                for name, cdt in [("convw", F32), ("convb", F32), ("dtb", F32), ("dtw", F32),
                                  ("dcol", F32), ("alog", F32), ("selrow16", BF16), ("wxpT", BF16)]:
                    shp = SBUF_SHAPE.get(name, list(WEIGHT_SHAPES[name]))
                    if name == "wxpT":
                        shp = [128, DIC, 33]
                    acst[name] = psb("c_" + name, shp, cdt)
                    src = dw[name].ap()
                    if name == "wxpT":
                        src = src.rearrange("(c p) o -> p c o", p=128)
                    eng_ = "pool" if cdt == BF16 else "sp"
                    b.op(eng_, lambda e, s_=src, n_=name: e.dma_start(out=acst[n_].ap(), in_=s_),
                         writes=[rcst], dma=True)
                A32 = psb("A32", [128, DIC, NST])
                wstr = psb("wstr", [128, 2, 4096], BF16)
                xnb = psb("xnb", [128, 2, DC, CB], BF16)
                xssm = psb("xssm", [128, DIC, CB + 3], BF16)
                zs = psb("zs", [128, DIC, CB], BF16)
                xc16 = psb("xc16", [128, DIC, CB], BF16)
                dt16 = psb("dt16", [128, DIC, CB], BF16)
                scr = psb("scr", [128, 16 * CB], BF16)
                dag = psb("dag", [128, 8, CB])
                brep = psb("brep", [128, NST, CB], BF16)
                crep = psb("crep", [128, NST, CB], BF16)
                h16 = psb("hh16", [128, NST, CB], BF16)
                g16 = psb("gg16", [128, DIC, CB], BF16)
                acc32 = psb("acc32", [128, 2, CB])
                bc16 = psb("bc16", [16, 2, CB], BF16)
                dtr32 = psb("dtr32", [1, CB])
                carry = psb("carry", [128, DIC, NST])
                yr32 = psb("yr32", [128, 2, CB])

                names = ["A32", "wstr0", "wstr1", "xnb0", "xnb1", "xssm", "zs", "xc16", "dt16",
                         "scr", "dag", "brep", "crep", "h16", "g16", "acc0", "acc1", "bc16",
                         "dtr32", "carry", "yr0", "yr1"]
                Rm = {n_: res("mb_" + n_) for n_ in names}
                b.seed(*Rm.values())

                emit_ln_stats()

                b.op("act", lambda e: e.activation(A32.ap(), acst["alog"].ap(), AF.Exp),
                     reads=[rcst], writes=[Rm["A32"]])
                b.op("dve", lambda e: e.tensor_scalar_mul(A32.ap(), A32.ap(), -1.0),
                     reads=[Rm["A32"]], writes=[Rm["A32"]])
                b.op("dve", lambda e: e.memset(carry.ap().rearrange("p a b -> p (a b)"), 0.0),
                     writes=[Rm["carry"]])
                b.op("dve", lambda e: e.memset(xssm.ap()[:, :, 0:3], 0.0), writes=[Rm["xssm"]])

                win_r = dw["winT"].ap().rearrange("(c p) o -> p c o", p=128)
                wout_r = dw["woutT"].ap().rearrange("(c p) o -> p c o", p=128)
                scr_b16 = scr.ap().rearrange("p (n t) -> p n t", n=16)

                def emit_inproj(sl_):
                    T0 = sl_ * CB
                    sp_ = sl_ % 2
                    for dc in range(DC):
                        emit_ln_apply(dc, (T0, T0 + CB), xnb.ap()[:, sp_, dc, :], dc % 2,
                                      Rm["xnb%d" % sp_])
                    if sl_ > 0:
                        b.op("dve", lambda e: e.tensor_copy(xssm.ap()[:, :, 0:3],
                                                            xssm.ap()[:, :, CB:CB + 3]),
                             reads=[Rm["xssm"]], writes=[Rm["xssm"]])
                    for og in range(8):
                        buf = og % 2
                        b.op("pool", lambda e, og=og, buf=buf: e.dma_start(
                            out=wstr.ap()[:, buf, :].rearrange("p (a b) -> p a b", a=DC),
                            in_=win_r[:, :, og * 512:(og + 1) * 512]),
                            writes=[Rm["wstr%d" % buf]], dma=True)
                        for j in range(4):
                            oc = og * 4 + j
                            pslot = oc % 2
                            for dc in range(DC):
                                b.op("pe", lambda e, buf=buf, j=j, dc=dc, pslot=pslot, sp_=sp_: e.matmul(
                                    pa(pslot, CB), wstr.ap()[:, buf, :].rearrange(
                                        "p (a b) -> p a b", a=DC)[:, dc, j * 128:(j + 1) * 128],
                                    xnb.ap()[:, sp_, dc, :], start=(dc == 0), stop=(dc == DC - 1)),
                                    reads=[Rm["wstr%d" % buf], Rm["xnb%d" % sp_], rcst],
                                    writes=[rpa[pslot]], inc=(dc == DC - 1))
                            if oc < 16:
                                b.op("act", lambda e, oc=oc, pslot=pslot: e.activation(
                                    xssm.ap()[:, oc, 3:3 + CB], pa(pslot, CB), AF.Identity),
                                    reads=[rpa[pslot]], writes=[Rm["xssm"]])
                            else:
                                b.op("act", lambda e, oc=oc, pslot=pslot: e.activation(
                                    zs.ap()[:, oc - 16, :], pa(pslot, CB), AF.Identity),
                                    reads=[rpa[pslot]], writes=[Rm["zs"]])

                def emit_ssm(sl_):
                    T0 = sl_ * CB
                    b.op("act", lambda e: e.activation(scr.ap(), zs.ap().rearrange("p a b -> p (a b)"),
                                                       AF.Sigmoid),
                         reads=[Rm["zs"]], writes=[Rm["scr"]])
                    b.op("pool", lambda e: e.tensor_mul(zs.ap().rearrange("p a b -> p (a b)"),
                                                        zs.ap().rearrange("p a b -> p (a b)"), scr.ap()),
                         reads=[Rm["zs"], Rm["scr"]], writes=[Rm["zs"]])
                    for dc in range(DIC):
                        ap_ = dc % 2
                        b.op("dve", lambda e, dc=dc, ap_=ap_: e.tensor_scalar(
                            acc32.ap()[:, ap_, :], xssm.ap()[:, dc, 3:3 + CB],
                            acst["convw"].ap()[:, dc, 3:4], None, ALU.mult),
                            reads=[Rm["xssm"], rcst], writes=[Rm["acc%d" % ap_]])
                        for kk in range(3):
                            b.op("dve", lambda e, dc=dc, kk=kk, ap_=ap_: e.scalar_tensor_tensor(
                                acc32.ap()[:, ap_, :], xssm.ap()[:, dc, kk:kk + CB],
                                acst["convw"].ap()[:, dc, kk:kk + 1], acc32.ap()[:, ap_, :],
                                ALU.mult, ALU.add),
                                reads=[Rm["xssm"], rcst, Rm["acc%d" % ap_]], writes=[Rm["acc%d" % ap_]])
                        b.op("dve", lambda e, dc=dc, ap_=ap_: e.tensor_scalar(
                            acc32.ap()[:, ap_, :], acc32.ap()[:, ap_, :],
                            acst["convb"].ap()[:, dc:dc + 1], None, ALU.add),
                            reads=[rcst, Rm["acc%d" % ap_]], writes=[Rm["acc%d" % ap_]])
                        b.op("act", lambda e, ap_=ap_: e.activation(
                            yr32.ap()[:, ap_, :], acc32.ap()[:, ap_, :], AF.Sigmoid),
                            reads=[Rm["acc%d" % ap_]], writes=[Rm["yr%d" % ap_]])
                        b.op("pool", lambda e, dc=dc, ap_=ap_: e.tensor_mul(
                            xc16.ap()[:, dc, :], acc32.ap()[:, ap_, :], yr32.ap()[:, ap_, :]),
                            reads=[Rm["acc%d" % ap_], Rm["yr%d" % ap_]], writes=[Rm["xc16"]])
                    for kc in range(DIC):
                        last = kc == DIC - 1
                        b.op("pe", lambda e, kc=kc: e.matmul(
                            pb(0, CB, (0, 1)), acst["wxpT"].ap()[:, kc, 0:1], xc16.ap()[:, kc, :],
                            start=(kc == 0), stop=(kc == DIC - 1)),
                            reads=[Rm["xc16"], rcst], writes=[rpb[0]], inc=last)
                        b.op("pe", lambda e, kc=kc: e.matmul(
                            pb(1, CB, (0, 16)), acst["wxpT"].ap()[:, kc, 1:17], xc16.ap()[:, kc, :],
                            start=(kc == 0), stop=(kc == DIC - 1)),
                            reads=[Rm["xc16"], rcst], writes=[rpb[1]], inc=last)
                        b.op("pe", lambda e, kc=kc: e.matmul(
                            pb(2, CB, (0, 16)), acst["wxpT"].ap()[:, kc, 17:33], xc16.ap()[:, kc, :],
                            start=(kc == 0), stop=(kc == DIC - 1)),
                            reads=[Rm["xc16"], rcst], writes=[rpb[2]], inc=last)
                    b.op("act", lambda e: e.activation(dtr32.ap(), pb(0, CB, (0, 1)), AF.Identity),
                         reads=[rpb[0]], writes=[Rm["dtr32"]])
                    b.op("act", lambda e: e.activation(bc16.ap()[:, 0, :], pb(1, CB, (0, 16)), AF.Identity),
                         reads=[rpb[1]], writes=[Rm["bc16"]])
                    b.op("act", lambda e: e.activation(bc16.ap()[:, 1, :], pb(2, CB, (0, 16)), AF.Identity),
                         reads=[rpb[2]], writes=[Rm["bc16"]])
                    for n_ in range(NST):
                        b.op("pe", lambda e, n_=n_: e.matmul(
                            pa(2, CB), acst["selrow16"].ap()[:, n_, :], bc16.ap()[:, 0, :],
                            start=True, stop=True),
                            reads=[Rm["bc16"], rcst], writes=[rpa[2]])
                        b.op("act", lambda e, n_=n_: e.activation(brep.ap()[:, n_, :], pa(2, CB), AF.Identity),
                             reads=[rpa[2]], writes=[Rm["brep"]])
                        b.op("pe", lambda e, n_=n_: e.matmul(
                            pa(2, CB), acst["selrow16"].ap()[:, n_, :], bc16.ap()[:, 1, :],
                            start=True, stop=True),
                            reads=[Rm["bc16"], rcst], writes=[rpa[2]])
                        b.op("act", lambda e, n_=n_: e.activation(crep.ap()[:, n_, :], pa(2, CB), AF.Identity),
                             reads=[rpa[2]], writes=[Rm["crep"]])
                    b.op("pe", lambda e: e.matmul(pb(3, CB), cst["ones_row32"].ap(), dtr32.ap(),
                                                  start=True, stop=True),
                         reads=[Rm["dtr32"], rcst], writes=[rpb[3]])
                    for dc in range(DIC):
                        b.op("dve", lambda e, dc=dc: e.scalar_tensor_tensor(
                            dt16.ap()[:, dc, :], pb(3, CB), acst["dtw"].ap()[:, dc:dc + 1],
                            acst["dtb"].ap()[:, dc:dc + 1].broadcast_to([128, CB]),
                            ALU.mult, ALU.add),
                            reads=[rpb[3], rcst], writes=[Rm["dt16"]])
                    dtf = dt16.ap().rearrange("p a b -> p (a b)")
                    b.op("act", lambda e: e.activation(scr.ap(), dtf, AF.Abs),
                         reads=[Rm["dt16"]], writes=[Rm["scr"]])
                    b.op("act", lambda e: e.activation(scr.ap(), scr.ap(), AF.Exp, scale=-1.0),
                         reads=[Rm["scr"]], writes=[Rm["scr"]])
                    b.op("act", lambda e: e.activation(scr.ap(), scr.ap(), AF.Ln, bias=1.0),
                         reads=[Rm["scr"]], writes=[Rm["scr"]])
                    b.op("act", lambda e: e.activation(g16.ap().rearrange("p a b -> p (a b)"), dtf, AF.Relu),
                         reads=[Rm["dt16"]], writes=[Rm["g16"]])
                    b.op("dve", lambda e: e.tensor_add(dtf, scr.ap(), g16.ap().rearrange("p a b -> p (a b)")),
                         reads=[Rm["scr"], Rm["g16"]], writes=[Rm["dt16"]])
                    for dc in range(DIC):
                        b.op("dve", lambda e, dc=dc: e.tensor_mul(
                            g16.ap()[:, dc, :], dt16.ap()[:, dc, :], xc16.ap()[:, dc, :]),
                            reads=[Rm["dt16"], Rm["xc16"]], writes=[Rm["g16"]])
                        for ng in range(2):
                            for nn in range(8):
                                n_ = ng * 8 + nn
                                b.op("act", lambda e, dc=dc, nn=nn, n_=n_: e.activation(
                                    dag.ap()[:, nn, :], dt16.ap()[:, dc, :], AF.Exp,
                                    scale=A32.ap()[:, dc, n_:n_ + 1]),
                                    reads=[Rm["dt16"], Rm["A32"]], writes=[Rm["dag"]])
                            b.op("dve", lambda e, dc=dc, ng=ng: e.tensor_mul(
                                scr_b16[:, 0:8, :],
                                g16.ap()[:, dc, :].unsqueeze(1).broadcast_to([128, 8, CB]),
                                brep.ap()[:, ng * 8:ng * 8 + 8, :]),
                                reads=[Rm["g16"], Rm["brep"], Rm["scr"]], writes=[Rm["scr"]])
                            for nn in range(8):
                                n_ = ng * 8 + nn
                                b.op("dve", lambda e, dc=dc, nn=nn, n_=n_: e.tensor_tensor_scan(
                                    h16.ap()[:, n_, :], dag.ap()[:, nn, :], scr_b16[:, nn, :],
                                    carry.ap()[:, dc, n_:n_ + 1], ALU.mult, ALU.add),
                                    reads=[Rm["dag"], Rm["scr"], Rm["carry"]], writes=[Rm["h16"]])
                        b.op("dve", lambda e, dc=dc: e.tensor_copy(
                            carry.ap()[:, dc, :], h16.ap()[:, :, CB - 1]),
                            reads=[Rm["h16"]], writes=[Rm["carry"]])
                        b.op("dve", lambda e: e.tensor_mul(scr_b16[:, :, :], h16.ap(), crep.ap()),
                             reads=[Rm["h16"], Rm["crep"], Rm["scr"]], writes=[Rm["scr"]])
                        b.op("dve", lambda e: e.tensor_reduce(
                            yr32.ap()[:, 0, :], scr_b16.transpose([0, 2, 1]), AX.X, ALU.add),
                            reads=[Rm["scr"]], writes=[Rm["yr0"]])
                        b.op("dve", lambda e, dc=dc: e.scalar_tensor_tensor(
                            yr32.ap()[:, 1, :], xc16.ap()[:, dc, :], acst["dcol"].ap()[:, dc:dc + 1],
                            yr32.ap()[:, 0, :], ALU.mult, ALU.add),
                            reads=[Rm["xc16"], rcst, Rm["yr0"]], writes=[Rm["yr1"]])
                        b.op("pool", lambda e, dc=dc: e.tensor_mul(
                            g16.ap()[:, dc, :], yr32.ap()[:, 1, :], zs.ap()[:, dc, :]),
                            reads=[Rm["yr1"], Rm["zs"], Rm["g16"]], writes=[Rm["g16"]])

                def emit_outproj(sl_):
                    T0 = sl_ * CB
                    for og in range(4):
                        buf = og % 2
                        b.op("pool", lambda e, og=og, buf=buf: e.dma_start(
                            out=wstr.ap()[:, buf, :].rearrange("p (a b) -> p a b", a=DIC),
                            in_=wout_r[:, :, og * 256:(og + 1) * 256]),
                            writes=[Rm["wstr%d" % buf]], dma=True)
                        for j in range(2):
                            oc = og * 2 + j
                            pslot = 2 + oc % 2
                            for kc in range(DIC):
                                b.op("pe", lambda e, buf=buf, j=j, kc=kc, pslot=pslot: e.matmul(
                                    pa(pslot, CB), wstr.ap()[:, buf, :].rearrange(
                                        "p (a b) -> p a b", a=DIC)[:, kc, j * 128:(j + 1) * 128],
                                    g16.ap()[:, kc, :], start=(kc == 0), stop=(kc == DIC - 1)),
                                    reads=[Rm["wstr%d" % buf], Rm["g16"], rcst], writes=[rpa[pslot]],
                                    inc=(kc == DIC - 1))
                            b.op("dve", lambda e, oc=oc, pslot=pslot, T0=T0: e.tensor_add(
                                x32.ap()[:, oc, T0:T0 + CB], x32.ap()[:, oc, T0:T0 + CB], pa(pslot, CB)),
                                reads=[rpa[pslot], rx], writes=[rx])

                emit_inproj(0)
                for sl_ in range(NSL):
                    emit_ssm(sl_)
                    if sl_ + 1 < NSL:
                        emit_inproj(sl_ + 1)
                    emit_outproj(sl_)

        if 3 in enabled:
            emit_attn_b()
        if 4 in enabled:
            emit_ffn("f2_")



        stok = b.op("sp", lambda e: e.dma_start(out=xout.ap().rearrange("(c p) t -> p c t", p=128),
                                                in_=x32.ap()),
                    reads=[rx], dma=True)
        for ln in DMA_LANES["sp"]:
            if b.c[ln]:
                nc.sync.wait_ge(sems[ln], b.c[ln])
        for ln in DMA_LANES["pool"]:
            if b.c[ln]:
                nc.gpsimd.wait_ge(sems[ln], b.c[ln])

    return nc


def kernel(**inputs):
    w = prep_weights(inputs)
    x = np.asarray(inputs["L0_attn_a_x"], dtype=np.float32)
    nc = build_nc(S=2048, upto=4)
    in_maps = []
    for c in range(NCORES):
        m = dict(w)
        m["xin"] = np.ascontiguousarray(x[c].T)
        in_maps.append(m)
    res = run_bass_kernel_spmd(nc, in_maps, core_ids=list(range(NCORES)))
    out = np.stack([np.asarray(r["xout"]).T for r in res.results], axis=0)
    return np.ascontiguousarray(out, dtype=np.float32)
